# version 03 (final)
# baseline (speedup 1.0000x reference)
import numpy as np
import ml_dtypes
import concourse.bass as bass
import concourse.mybir as mybir
from concourse.bass_utils import run_bass_kernel_spmd
from concourse.bass import IndirectOffsetOnAxis

F32 = mybir.dt.float32
BF16 = mybir.dt.bfloat16
I32 = mybir.dt.int32
U32 = mybir.dt.uint32
AF = mybir.ActivationFunctionType
ALU = mybir.AluOpType
AX = mybir.AxisListType
BF = ml_dtypes.bfloat16

NCORES = 8
T = 2048
D = 1024
NCH = 8
CH = 128
EPS = 1e-5


class Sem:
    __slots__ = ("name", "h", "v")
    def __init__(self, name, h):
        self.name, self.h, self.v = name, h, 0


class Buf:
    __slots__ = ("name", "w", "rs")
    def __init__(self, name):
        self.name, self.w, self.rs = name, None, []


class Prog:
    ENGS = ("t", "v", "s", "g", "y")

    def __init__(self, nc):
        self.nc = nc
        self.q = {e: [] for e in self.ENGS}
        self.hw = {e: {} for e in self.ENGS}
        self.esem = {}
        self.nins = 0
        self.allsems = set()

    def wait(self, e, sem, val):
        if sem is None or val is None:
            return
        if val <= self.hw[e].get(sem.name, 0):
            return
        assert val <= sem.v, f"wait on future {sem.name}: {val} > {sem.v}"
        self.hw[e][sem.name] = val
        self.q[e].append(lambda eng, s=sem.h, v=val: eng.wait_ge(s, v))

    def ins(self, e, make, reads=(), writes=(), sem=None, inc=None):
        if sem is None:
            sem = self.esem[e]
            inc = 16 if e == "y" else 1
        if inc is None:
            inc = 1
        for b in reads:
            if b.w is not None:
                self.wait(e, *b.w)
        for b in writes:
            if b.w is not None:
                self.wait(e, *b.w)
            for r in b.rs:
                self.wait(e, *r)
        sem.v += inc
        self.allsems.add(sem)
        self.q[e].append(lambda eng, mk=make, s=sem.h, i=inc: mk(eng).then_inc(s, i))
        self.nins += 1
        tok = (sem, sem.v)
        for b in reads:
            b.rs.append(tok)
        for b in writes:
            b.w = tok
            b.rs = []
        return tok

    def dmas(self, e, makes, reads=(), writes=(), sem=None):
        if sem is None:
            sem = self.esem["y"]
        for b in reads:
            if b.w is not None:
                self.wait(e, *b.w)
        for b in writes:
            if b.w is not None:
                self.wait(e, *b.w)
            for r in b.rs:
                self.wait(e, *r)
        self.allsems.add(sem)
        for mk in makes:
            sem.v += 16
            self.q[e].append(lambda eng, m=mk, s=sem.h: m(eng).then_inc(s, 16))
            self.nins += 1
        tok = (sem, sem.v)
        for b in reads:
            b.rs.append(tok)
        for b in writes:
            b.w = tok
            b.rs = []
        return tok

    def fence(self, e, ap, buf, scratch):
        if e == "s":
            self.ins("s", lambda eng: eng.activation(scratch, ap, AF.Copy),
                     reads=[buf], writes=[])
        else:
            self.ins("v", lambda eng: eng.tensor_copy(scratch, ap),
                     reads=[buf], writes=[])

    def barrier(self):
        sems = set(self.allsems)
        for e in self.ENGS:
            sems.add(self.esem[e])
        vals = [(s, s.v) for s in sems]
        for e in self.ENGS:
            for s, v in vals:
                self.wait(e, s, v)

    def replay(self, block):
        def mk(tag):
            def runner(eng):
                for f in self.q[tag]:
                    f(eng)
            return runner
        block.tensor(mk("t"))
        block.vector(mk("v"))
        block.scalar(mk("s"))
        block.gpsimd(mk("g"))
        block.sync(mk("y"))


class Core:
    pass


def _sb(C, st, name, shape, dt):
    t = st.enter_context(C.nc.sbuf_tensor("sb_" + name, shape, dt))
    return t, Buf(name)


def _ps(C, st, name, shape, dt=F32):
    t = st.enter_context(C.nc.psum_tensor(name, shape, dt))
    return t, Buf(name)


def alloc_common(nc, st, P):
    C = Core()
    C.nc = nc
    C.ccstack = st
    for nm in ("dY", "dS", "dG", "tM", "vV", "sS", "gG", "cC", "dA", "dB",
               "dC", "dD", "dE", "dF", "dH", "dI", "dJ", "dK", "dL", "dM",
               "dN", "dO"):
        setattr(C, nm, Sem(nm, st.enter_context(nc.semaphore(nm))))
    P.esem.update({"t": C.tM, "v": C.vV, "s": C.sS, "g": C.gG, "y": C.dY})
    C.ident, C.b_ident = _sb(C, st, "sb_ident", [128, 128], BF16)
    C.ones_row, _ = _sb(C, st, "sb_ones", [1, 128], F32)
    C.ones_col, _ = _sb(C, st, "sb_onescol", [128, 2], BF16)
    C.consts, C.b_consts = _sb(C, st, "sb_consts", [1, 8], F32)
    C.onesb, _ = _sb(C, st, "sb_onesb", [1, 512], BF16)
    C.bones, _ = _sb(C, st, "sb_bones", [128, 2], BF16)
    C.b_ones = Buf("c_onesX")
    C.eps = C.consts[0:1, 0:1]
    C.fscr, _ = _sb(C, st, "sb_fscr", [1, 4], F32)
    C.identf, _ = _sb(C, st, "sb_identf", [128, 128], F32)
    return C


def load_common(C, P, hb):
    C.hzeros = hb["c_zeros"]
    P.dmas("y", [
        lambda eng: eng.dma_start(out=C.ident[:], in_=hb["c_ident"]),
        lambda eng: eng.dma_start(out=C.ones_row[:], in_=hb["c_ones"]),
        lambda eng: eng.dma_start(out=C.consts[:], in_=hb["c_consts"]),
        lambda eng: eng.dma_start(out=C.ones_col[:], in_=hb["c_onescol"]),
        lambda eng: eng.dma_start(out=C.onesb[:], in_=hb["c_onesb"]),
        lambda eng: eng.dma_start(out=C.bones[:], in_=hb["c_bones"]),
        lambda eng: eng.dma_start(out=C.identf[:], in_=hb["c_identf"]),
    ], writes=[C.b_ident, C.b_ones, C.b_consts], sem=C.dK)
    P.barrier()


def common_host_inputs():
    cst = np.zeros((1, 8), np.float32)
    cst[0, 0] = EPS
    return {
        "c_ident": np.eye(128, dtype=np.float32).astype(BF),
        "c_ones": np.ones((1, 128), dtype=np.float32),
        "c_onescol": np.ones((128, 2), dtype=np.float32).astype(BF),
        "c_consts": cst,
        "c_onesb": np.ones((1, 512), dtype=np.float32).astype(BF),
        "c_zeros": np.zeros((1024, 4608), dtype=np.float32).astype(BF),
        "c_identf": np.eye(128, dtype=np.float32),
        "c_bones": np.concatenate([
            np.concatenate([np.ones((64, 1)), np.zeros((64, 1))], 0),
            np.concatenate([np.zeros((64, 1)), np.ones((64, 1))], 0)], 1
        ).astype(np.float32).astype(BF),
    }



def _tiles_fm(M):
    M = np.asarray(M, dtype=np.float32)
    di, do = M.shape
    KB = (di + 127) // 128
    FB = (do + 127) // 128
    Pd = np.zeros((KB * 128, FB * 128), dtype=np.float32)
    Pd[:di, :do] = M
    return np.ascontiguousarray(Pd.reshape(KB, 128, FB, 128).transpose(2, 0, 1, 3))


def _fmcol(v):
    v = np.asarray(v, dtype=np.float32)
    return np.ascontiguousarray(v.reshape(NCH, 128).T)


def _fm_x(x):
    x = np.asarray(x, dtype=np.float32)
    t = x.shape[0]
    return np.ascontiguousarray(x.T.reshape(NCH, 128, t))



def alloc_lnws(C, st):
    ws = {}
    ws["sq"], ws["b_sq"] = _sb(C, st, "ln_sq", [128, 512], BF16)
    ws["xb"], ws["b_xb"] = _sb(C, st, "ln_xb", [128, 512], BF16)
    ws["srow_r"], _ = _sb(C, st, "ln_srow_r", [1, 512], F32)
    ws["srow_n"], ws["b_srow"] = _sb(C, st, "ln_srow_n", [1, 512], F32)
    ws["stxa"], _ = _sb(C, st, "ln_stxa", [1, 512], F32)
    ws["stxb"], ws["b_stx"] = _sb(C, st, "ln_stxb", [1, 512], F32)
    ws["tmp"], ws["b_tmp"] = _sb(C, st, "ln_tmp", [128, 512], F32)
    ws["ps_st"], ws["b_ps_st"] = _ps(C, st, "ln_ps_st", [33, 512])
    return ws


def emit_ln(C, P, ws, x_ap, x_bufs, out_ap, out_bufs, pbc, n=T,
            src_bf16=False, gamma=None, beta=None, b_gb=None,
            accum_ap=None, accum_bufs=None, accum_scale=None,
            pre_tile=None, tile_cb=None, tstep=512, out_local=False, ntiles_limit=None):
    (bc0, b_bc0), (bc1, b_bc1) = pbc
    sq, b_sq = ws["sq"], ws["b_sq"]
    xb, b_xb = ws["xb"], ws["b_xb"]
    srow_r, srow_n, b_srow = ws["srow_r"], ws["srow_n"], ws["b_srow"]
    stxa, stxb, b_stx = ws["stxa"], ws["stxb"], ws["b_stx"]
    tmp, b_tmp = ws["tmp"], ws["b_tmp"]
    ps_st, b_ps_st = ws["ps_st"], ws["b_ps_st"]
    assert tstep <= 512
    nt = (n + tstep - 1) // tstep
    if ntiles_limit is not None:
        nt = min(nt, ntiles_limit)
    for tt in range(nt):
        w = min(tstep, n - tt * tstep)
        sl = slice(tt * tstep, tt * tstep + w)
        if pre_tile is not None:
            x_fn, xtb = pre_tile(tt, sl, w)
        else:
            x_fn = lambda c, sl=sl: x_ap(c)[:, sl]
            xtb = x_bufs
        for c in range(NCH):
            if not src_bf16:
                P.ins("v", lambda eng, c=c, w=w, x_fn=x_fn: eng.tensor_copy(xb[:, 0:w], x_fn(c)),
                      reads=[xtb[c]], writes=[b_xb])
            sumsrc = (x_fn(c), xtb[c]) if src_bf16 else (xb[:, 0:w], b_xb)
            P.ins("s", lambda eng, a=sumsrc[0], w=w: eng.activation(sq[:, 0:w], a, AF.Square),
                  reads=[sumsrc[1]], writes=[b_sq])
            def mk(eng, a=sumsrc[0], c=c, w=w):
                eng.matmul(ps_st[0:1, 0:w], lhsT=C.ones_col[:, 0:1], rhs=a,
                           start=(c == 0), stop=(c == NCH - 1))
                return eng.matmul(ps_st[32:33, 0:w], lhsT=C.ones_col[:, 1:2], rhs=sq[:, 0:w],
                                  start=(c == 0), stop=(c == NCH - 1))
            P.ins("t", mk, reads=[sumsrc[1], b_sq, C.b_ones], writes=[b_ps_st])
        def mk_mu(eng, w=w):
            eng.tensor_scalar(stxa[0:1, 0:w], ps_st[0:1, 0:w], 1.0 / D, None, ALU.mult)
            return eng.tensor_scalar(stxb[0:1, 0:w], ps_st[32:33, 0:w], 1.0 / D, None, ALU.mult)
        P.ins("v", mk_mu, reads=[b_ps_st], writes=[b_stx])
        P.ins("s", lambda eng, w=w: eng.activation(srow_r[0:1, 0:w], stxa[0:1, 0:w], AF.Square),
              reads=[b_stx], writes=[b_srow])
        P.ins("v", lambda eng, w=w: eng.tensor_tensor(srow_r[0:1, 0:w], stxb[0:1, 0:w],
                                                      srow_r[0:1, 0:w], ALU.subtract),
              reads=[b_stx, b_srow], writes=[b_srow])
        P.ins("s", lambda eng, w=w: eng.activation(srow_r[0:1, 0:w], srow_r[0:1, 0:w], AF.Ln,
                                                   bias=C.eps),
              reads=[b_srow, C.b_ones], writes=[b_srow])
        P.ins("s", lambda eng, w=w: eng.activation(srow_r[0:1, 0:w], srow_r[0:1, 0:w], AF.Exp,
                                                   scale=-0.5),
              reads=[b_srow], writes=[b_srow])
        def mk_nmr(eng, w=w):
            eng.tensor_tensor(srow_n[0:1, 0:w], stxa[0:1, 0:w], srow_r[0:1, 0:w], ALU.mult)
            return eng.tensor_scalar(srow_n[0:1, 0:w], srow_n[0:1, 0:w], -1.0, None, ALU.mult)
        P.ins("v", mk_nmr, reads=[b_stx, b_srow], writes=[b_srow])
        P.ins("t", lambda eng, w=w: eng.matmul(bc0[:, 0:w], lhsT=C.ones_row[0:1, :],
                                               rhs=srow_r[0:1, 0:w], start=True, stop=True),
              reads=[b_srow, C.b_ones], writes=[b_bc0])
        P.ins("t", lambda eng, w=w: eng.matmul(bc1[:, 0:w], lhsT=C.ones_row[0:1, :],
                                               rhs=srow_n[0:1, 0:w], start=True, stop=True),
              reads=[b_srow, C.b_ones], writes=[b_bc1])
        for c in range(NCH):
            P.ins("v", lambda eng, c=c, w=w, x_fn=x_fn: eng.tensor_tensor(
                      tmp[:, 0:w], x_fn(c), bc0[:, 0:w], ALU.mult),
                  reads=[xtb[c], b_bc0], writes=[b_tmp])
            osl = (lambda c, sl=sl, w=w: out_ap(c)[:, 0:w] if out_local else out_ap(c)[:, sl])
            if gamma is None and accum_ap is None:
                P.ins("v", lambda eng, c=c, w=w, osl=osl: eng.tensor_tensor(
                          osl(c), tmp[:, 0:w], bc1[:, 0:w], ALU.add),
                      reads=[b_tmp, b_bc1], writes=[out_bufs[c]])
                continue
            P.ins("v", lambda eng, c=c, w=w: eng.tensor_tensor(
                      tmp[:, 0:w], tmp[:, 0:w], bc1[:, 0:w], ALU.add),
                  reads=[b_tmp, b_bc1], writes=[b_tmp])
            if gamma is not None:
                P.ins("v", lambda eng, c=c, w=w: eng.tensor_scalar(
                          tmp[:, 0:w], tmp[:, 0:w], gamma(c), beta(c), ALU.mult, ALU.add),
                      reads=[b_tmp, b_gb], writes=[b_tmp])
            if out_ap is not None:
                P.ins("v", lambda eng, c=c, w=w, osl=osl: eng.tensor_copy(osl(c), tmp[:, 0:w]),
                      reads=[b_tmp], writes=[out_bufs[c]])
            if accum_ap is not None:
                if accum_scale is None:
                    P.ins("v", lambda eng, c=c, sl=sl, w=w: eng.tensor_tensor(
                              accum_ap(c)[:, sl], accum_ap(c)[:, sl], tmp[:, 0:w], ALU.add),
                          reads=[b_tmp], writes=[accum_bufs[c]])
                else:
                    P.ins("v", lambda eng, c=c, sl=sl, w=w: eng.scalar_tensor_tensor(
                              accum_ap(c)[:, sl], tmp[:, 0:w], accum_scale,
                              accum_ap(c)[:, sl], ALU.mult, ALU.add),
                          reads=[b_tmp], writes=[accum_bufs[c]])
        if tile_cb is not None:
            tile_cb(tt, sl, w)



def emit_allgather(C, P, name, src_sbuf_ap, b_src, dst_shape, dt=F32, b_extra=()):
    nc = C.nc
    n = dst_shape[1]
    agi = nc.dram_tensor(f"{name}_agi", [128 * NCORES, n], dt).ap()
    ago = nc.dram_tensor(f"{name}_ago", [128 * NCORES, n], dt).ap()
    b_agi, b_ago = Buf(f"{name}_agi"), Buf(f"{name}_ago")
    if dt == BF16:
        zsrc = C.hzeros[:, 0:n]
    else:
        zsrc = C.hzeros.bitcast(F32)[:, 0:n]
    isem = Sem(f"{name}_in", C.ccstack.enter_context(nc.semaphore(f"{name}_in")))
    P.ins("y", lambda eng: eng.dma_start(out=agi, in_=zsrc),
          writes=[b_agi], sem=isem, inc=16)
    pid = nc.partition_id()
    P.ins("y", lambda eng: eng.dma_start(out=agi[bass.ts(pid, 128), :], in_=src_sbuf_ap),
          reads=[b_src] + list(b_extra), writes=[b_agi], sem=isem, inc=16)
    csem = Sem(f"{name}_cc", C.ccstack.enter_context(nc.semaphore(f"{name}_cc")))
    P.ins("g", lambda eng: eng.collective_compute(
              "AllReduce", ALU.add, replica_groups=[list(range(NCORES))],
              ins=[agi.opt()], outs=[ago.opt()]),
          reads=[b_agi], writes=[b_ago], sem=csem, inc=1)
    return ago, b_ago



def a0_host_shared(inp):
    d = {}
    for nm in ("W_r", "W_k", "W_v", "W_g", "W_o"):
        d[f"a0_{nm}"] = _tiles_fm(np.asarray(inp[f"L0_attn_{nm}"]).T)
    d["a0_tmix"] = np.stack([_fmcol(inp[f"L0_attn_time_mix_{s}"]) for s in "kvrg"], 1)
    d["a0_dcol"] = _fmcol(inp["L0_attn_time_decay"])
    d["a0_drow"] = np.asarray(inp["L0_attn_time_decay"], np.float32).reshape(1, D)
    d["a0_lng"] = _fmcol(inp["L0_attn_ln_gamma"])
    d["a0_lnb"] = _fmcol(inp["L0_attn_ln_beta"])
    d["a0_qrow"] = np.tile(np.arange(CH, dtype=np.float32)[None, :] - 64.0, (128, 1))
    d["a0_pcol"] = (np.arange(128, dtype=np.float32)[:, None] - 64.0).astype(np.float32)
    mq = np.arange(CH)[:, None]
    mp = np.arange(CH)[None, :]
    d["a0_mask"] = (mq <= mp).astype(np.float32).astype(BF)
    return d


def a0_host_percore(core):
    d = {}
    w = np.zeros((128, 8), dtype=np.float32)
    if core > 0:
        w[:, core - 1] = 1.0
    d["a0_stflag"] = w
    d["a0_fself"] = np.full((128, 1), 1.0 if core == 0 else 0.0, np.float32)
    return d


def emit_a0(C, P, st, hb, xr, b_xr, lnws, skip_fix=False):
    nc = C.nc
    H, DH = 16, 64
    NT = T // 512

    y, _ = _sb(C, st, "a0_y", [128, NCH, T + 1], BF16)
    b_y = [Buf(f"a0_y{c}") for c in range(NCH)]
    dl, b_dl = _sb(C, st, "a0_d", [128, NCH, 512], BF16)
    mx, b_mx = _sb(C, st, "a0_mix", [128, NCH, 512], BF16)
    rkv, _ = _sb(C, st, "a0_rkv", [128, 3, NCH, 512], BF16)
    b_rkv = [Buf(f"a0_rkv{m}") for m in range(3)]
    gt, b_gt = _sb(C, st, "a0_g", [128, NCH, 512], BF16)
    g0, b_g0 = _sb(C, st, "a0_g0", [128, NCH, CH], BF16)
    r0, b_r0 = _sb(C, st, "a0_r0", [128, NCH, CH], BF16)
    att, b_att = _sb(C, st, "a0_att", [128, NCH, 512], BF16)
    atok, b_atok = _sb(C, st, "a0_atok", [128, 1024], BF16)
    ktm, b_ktm = _sb(C, st, "a0_ktm", [128, 1024], BF16)
    vtm, b_vtm = _sb(C, st, "a0_vtm", [128, 1024], BF16)
    kt8, b_kt8 = _sb(C, st, "a0_kt8", [128, NCH, CH], BF16)
    ath, b_ath = _sb(C, st, "a0_ath", [128, CH], BF16)
    zs, b_zs = _sb(C, st, "a0_zs", [64, H * DH], F32)
    zsb, b_zsb = _sb(C, st, "a0_zsb", [128, H * DH], BF16)
    agst, b_agst = _sb(C, st, "a0_agst", [128, H * DH], F32)
    zac, b_zac = agst, b_agst
    tmix, b_tmix = _sb(C, st, "a0_tmix", [128, 4, NCH], F32)
    ucol, b_ucol = _sb(C, st, "a0_ucol", [128, NCH], F32)
    urow, b_urow = _sb(C, st, "a0_urow", [1, D], F32)
    qrow, b_qrow = _sb(C, st, "a0_qrow", [128, CH], F32)
    pcol, b_pcol = _sb(C, st, "a0_pcol", [128, 1], F32)
    wneg, b_wneg = _sb(C, st, "a0_wneg", [128, NCH, CH], BF16)
    wpos, b_wpos = _sb(C, st, "a0_wpos", [128, D], BF16)
    w64b, b_w64b = _sb(C, st, "a0_w64b", [128, D], F32)
    fsm, b_fsm = _sb(C, st, "a0_fsm", [128, CH], F32)
    msk, b_msk = _sb(C, st, "a0_msk", [128, CH], BF16)
    lng, b_lng = _sb(C, st, "a0_lng", [128, NCH], F32)
    lnb, _ = _sb(C, st, "a0_lnb", [128, NCH], F32)
    stf, b_stf = _sb(C, st, "a0_stf", [128, 8], F32)
    fse, b_fse = _sb(C, st, "a0_fse", [128, 1], F32)
    c0s, b_c0s = _sb(C, st, "a0_c0s", [128, NCH], F32)
    ycb, b_ycb = _sb(C, st, "a0_ycb", [128, NCH, NT], BF16)
    ps_mm = [_ps(C, st, f"a0_mm{i}", [128, 512]) for i in range(2)]
    ps_mm += [_ps(C, st, f"a0_out{i}", [128, 512]) for i in range(2)]
    (ps_o0, b_ps_o0), (ps_o1, b_ps_o1) = ps_mm[2], ps_mm[3]
    ps_tr, b_ps_tr = _ps(C, st, "a0_tr", [128, 1024], BF16)
    ps_u, b_ps_u = _ps(C, st, "a0_u", [64, 1024])

    wdb = [_sb(C, st, f"a0_w{i}", [128, NCH, 128], BF16) for i in range(2)]
    wslot = [0]

    def wload(mat_ap, f):
        i = wslot[0]
        wslot[0] ^= 1
        wt, wb = wdb[i]
        P.ins("g", lambda eng, f=f: eng.dma_start(
                  out=wt[:, :, :], in_=mat_ap[f].rearrange("k p f -> p k f")),
              writes=[wb], sem=(C.dA if i == 0 else C.dB), inc=16)
        return wt, wb

    P.dmas("y", [
        lambda eng: eng.dma_start(out=tmix[:], in_=hb["a0_tmix"]),
        lambda eng: eng.dma_start(out=ucol[:], in_=hb["a0_dcol"]),
        lambda eng: eng.dma_start(out=urow[:], in_=hb["a0_drow"]),
        lambda eng: eng.dma_start(out=qrow[:], in_=hb["a0_qrow"]),
        lambda eng: eng.dma_start(out=pcol[:], in_=hb["a0_pcol"]),
        lambda eng: eng.dma_start(out=msk[:], in_=hb["a0_mask"]),
        lambda eng: eng.dma_start(out=lng[:], in_=hb["a0_lng"]),
        lambda eng: eng.dma_start(out=lnb[:], in_=hb["a0_lnb"]),
        lambda eng: eng.dma_start(out=stf[:], in_=hb["a0_stflag"]),
        lambda eng: eng.dma_start(out=fse[:], in_=hb["a0_fself"]),
    ], writes=[b_tmix, b_ucol, b_urow, b_qrow, b_pcol, b_msk,
               b_lng, b_stf, b_fse], sem=C.dL)
    P.barrier()

    P.ins("s", lambda eng: eng.activation(ucol[:], ucol[:], AF.Exp),
          reads=[b_ucol], writes=[b_ucol])
    P.ins("s", lambda eng: eng.activation(urow[:], urow[:], AF.Exp),
          reads=[b_urow], writes=[b_urow])
    for c in range(NCH):
        P.ins("v", lambda eng, c=c: eng.tensor_scalar(
                  fsm[:, :], qrow[:, :], ucol[:, c:c + 1], None, ALU.mult),
              reads=[b_qrow, b_ucol], writes=[b_fsm])
        P.ins("s", lambda eng, c=c: eng.activation(wneg[:, c, :], fsm[:, :], AF.Exp),
              reads=[b_fsm], writes=[b_wneg])
    ubc = mx.ap().bitcast(F32).rearrange("p c q -> p (c q)")[:, 0:D]
    b_ubc = b_mx
    for h2 in range(2):
        pso, bpso = ps_mm[2 + h2]
        P.ins("t", lambda eng, h2=h2, pso=pso: eng.matmul(
                  pso[:, :], lhsT=C.ones_row[0:1, :],
                  rhs=urow[0:1, h2 * 512:(h2 + 1) * 512], start=True, stop=True),
              reads=[b_urow, C.b_ones], writes=[bpso])
        P.ins("v", lambda eng, h2=h2, pso=pso: eng.tensor_copy(ubc[:, h2 * 512:(h2 + 1) * 512],
                                                               pso[:, :]),
              reads=[bpso], writes=[b_ubc])
    P.ins("v", lambda eng: eng.tensor_scalar(w64b[:], ubc, pcol[:, 0:1], None, ALU.mult),
          reads=[b_pcol, b_ubc], writes=[b_w64b])
    P.ins("s", lambda eng: eng.activation(wpos[:], w64b[:], AF.Exp, scale=-1.0),
          reads=[b_w64b], writes=[b_wpos])
    P.ins("s", lambda eng: eng.activation(w64b[:], ubc, AF.Exp, scale=-64.0),
          reads=[b_ubc], writes=[b_w64b])

    emit_ln(C, P, lnws, lambda c: xr[:, c, :], b_xr,
            lambda c: y[:, c, 1:T + 1], b_y, ps_mm[2:4])

    def mk_ycb(eng):
        last = None
        for tt2 in range(NT):
            for c in range(NCH):
                last = eng.tensor_copy(ycb[:, c, tt2:tt2 + 1], y[:, c, tt2 * 512:tt2 * 512 + 1])
        return last
    P.ins("v", mk_ycb, reads=b_y, writes=[b_ycb])

    P.ins("v", lambda eng: eng.tensor_copy(agst[:, 0:NCH], y[:, :, T:T + 1].squeeze(2)),
          reads=b_y, writes=[b_agst])
    ago_y, b_ago_y = emit_allgather(C, P, "a0y", agst[:, 0:NCH], b_agst, [128, NCH])
    P.ins("v", lambda eng: eng.memset(c0s[:], 0.0), writes=[b_c0s])
    ldj, b_ldj = _sb(C, st, "a0_ldj", [128, NCH], F32)
    for j in range(NCORES):
        P.ins("y", lambda eng, j=j: eng.dma_start(out=ldj[:], in_=ago_y[j * 128:(j + 1) * 128, :]),
              reads=[b_ago_y], writes=[b_ldj], sem=C.dF, inc=16)
        P.ins("v", lambda eng, j=j: eng.scalar_tensor_tensor(
                  c0s[:], ldj[:], stf[:, j:j + 1], c0s[:], ALU.mult, ALU.add),
              reads=[b_ldj, b_stf], writes=[b_c0s])
    P.ins("v", lambda eng: eng.scalar_tensor_tensor(
              c0s[:], y[:, :, 2:3].squeeze(2), fse[:, 0:1], c0s[:], ALU.mult, ALU.add),
          reads=[b_y[0], b_fse], writes=[b_c0s])
    P.ins("v", lambda eng: eng.tensor_copy(y[:, :, 0:1].squeeze(2), c0s[:]),
          reads=[b_c0s], writes=b_y)

    P.ins("v", lambda eng: eng.memset(zs[:], 0.0), writes=[b_zs])
    P.ins("v", lambda eng: eng.memset(zsb[:], 0.0), writes=[b_zsb])

    MATS = ("a0_W_r", "a0_W_k", "a0_W_v", "a0_W_g")
    for tt in range(NT):
        base = 1 + tt * 512
        for c in range(NCH):
            def mk_d(eng, c=c, base=base, tt=tt):
                if tt == 0:
                    eng.tensor_tensor(dl[:, c, 0:1], c0s[:, c:c + 1],
                                      y[:, c, base:base + 1], ALU.subtract)
                else:
                    eng.tensor_tensor(dl[:, c, 0:1], ycb[:, c, tt:tt + 1],
                                      y[:, c, base:base + 1], ALU.subtract)
                return eng.tensor_tensor(dl[:, c, 1:512], y[:, c, base:base + 511],
                                         y[:, c, base + 1:base + 512], ALU.subtract)
            P.ins("v", mk_d, reads=[b_y[c], b_ycb, b_c0s], writes=[b_dl])
        for m in range(4):
            for c in range(NCH):
                P.ins("v", lambda eng, c=c, m=m, base=base: eng.scalar_tensor_tensor(
                          mx[:, c, :], dl[:, c, :], tmix[:, m, c:c + 1],
                          y[:, c, base:base + 512], ALU.mult, ALU.add),
                      reads=[b_dl, b_y[c], b_tmix], writes=[b_mx])
            matap = hb[("a0_W_k", "a0_W_v", "a0_W_r", "a0_W_g")[m]]
            for f in range(NCH):
                wt, wb = wload(matap, f)
                ps, bps = ps_mm[f % 4]
                def mkp(eng, wt=wt, f=f, ps=ps):
                    last = None
                    for k in range(NCH):
                        last = eng.matmul(ps[:, :], lhsT=wt[:, k, :], rhs=mx[:, k, :],
                                          start=(k == 0), stop=(k == NCH - 1))
                    return last
                P.ins("t", mkp, reads=[wb, b_mx], writes=[bps])
                if m < 3:
                    dst = {0: 1, 1: 2, 2: 0}[m]
                    P.ins("s", lambda eng, f=f, ps=ps, dst=dst: eng.activation(
                              rkv[:, dst, f, :], ps[:, :], AF.Copy),
                          reads=[bps], writes=[b_rkv[dst]])
                else:
                    P.ins("s", lambda eng, f=f, ps=ps: eng.activation(
                              gt[:, f, :], ps[:, :], AF.Sigmoid),
                          reads=[bps], writes=[b_gt])
        if tt == 0:
            P.ins("v", lambda eng: eng.tensor_copy(r0[:, :, :], rkv[:, 0, :, 0:CH]),
                  reads=[b_rkv[0]], writes=[b_r0])
            P.ins("v", lambda eng: eng.tensor_copy(g0[:, :, :], gt[:, :, 0:CH]),
                  reads=[b_gt], writes=[b_g0])
        for q4 in range(4):
            qs = slice(q4 * CH, (q4 + 1) * CH)
            for c in range(NCH):
                P.ins("v", lambda eng, c=c, qs=qs: eng.tensor_tensor(
                          kt8[:, c, :], rkv[:, 1, c, qs], wneg[:, c, :], ALU.mult),
                      reads=[b_rkv[1], b_wneg], writes=[b_kt8])
            def mk_trk(eng):
                last = None
                for c in range(NCH):
                    last = eng.transpose(ps_tr[:, c * 128:(c + 1) * 128], kt8[:, c, :], C.ident[:])
                return last
            P.ins("t", mk_trk, reads=[b_kt8, C.b_ident], writes=[b_ps_tr])
            P.ins("v", lambda eng: eng.tensor_copy(ktm[:], ps_tr[:, :]),
                  reads=[b_ps_tr], writes=[b_ktm])
            def mk_trv(eng, qs=qs):
                last = None
                for c in range(NCH):
                    last = eng.transpose(ps_tr[:, c * 128:(c + 1) * 128], rkv[:, 2, c, qs], C.ident[:])
                return last
            P.ins("t", mk_trv, reads=[b_rkv[2], C.b_ident], writes=[b_ps_tr])
            P.ins("v", lambda eng: eng.tensor_copy(vtm[:], ps_tr[:, :]),
                  reads=[b_ps_tr], writes=[b_vtm])
            for h in range(H):
                c8, hb64 = h // 2, 64 * (h % 2)
                ps, bps = ps_mm[h % 2]
                P.ins("t", lambda eng, c8=c8, hb64=hb64, qs=qs, ps=ps: eng.matmul(
                          ps[:, 0:CH], lhsT=rkv[hb64:hb64 + 64, 2, c8, qs],
                          rhs=rkv[hb64:hb64 + 64, 0, c8, qs], start=True, stop=True),
                      reads=[b_rkv[2], b_rkv[0]], writes=[bps])
                P.ins("v", lambda eng, ps=ps: eng.tensor_tensor(ath[:, :], ps[:, 0:CH], msk[:, :],
                                                                ALU.mult),
                      reads=[bps, b_msk], writes=[b_ath])
                pso, bpso = ps_mm[2 + h // 8]
                off = (h % 8) * 64
                def mk_io(eng, h=h, c8=c8, hb64=hb64, qs=qs, pso=pso, off=off):
                    eng.matmul(pso[:, off:off + 64], lhsT=ath[:, :],
                               rhs=ktm[:, h * 64:(h + 1) * 64], start=True, stop=False)
                    return eng.matmul(pso[:, off:off + 64],
                                      lhsT=rkv[hb64:hb64 + 64, 0, c8, qs],
                                      rhs=zsb[hb64:hb64 + 64, h * 64:(h + 1) * 64],
                                      start=False, stop=True)
                P.ins("t", mk_io, reads=[b_ath, b_ktm, b_rkv[0], b_zsb], writes=[bpso])
            def mk_u(eng):
                last = None
                for h in range(H):
                    last = eng.matmul(ps_u[0:64, h * 64:(h + 1) * 64], lhsT=vtm[:, h * 64:(h + 1) * 64],
                                      rhs=ktm[:, h * 64:(h + 1) * 64], start=True, stop=True)
                return last
            P.ins("t", mk_u, reads=[b_vtm, b_ktm], writes=[b_ps_u])
            for h2 in range(2):
                pso, bpso = ps_mm[2 + h2]
                P.ins("v", lambda eng, h2=h2, pso=pso: eng.tensor_tensor(
                          atok[:, h2 * 512:(h2 + 1) * 512], pso[:, :],
                          wpos[:, h2 * 512:(h2 + 1) * 512], ALU.mult),
                      reads=[bpso, b_wpos], writes=[b_atok])
            def mk_trb(eng):
                last = None
                for c in range(NCH):
                    last = eng.transpose(ps_tr[:, c * 128:(c + 1) * 128], atok[:, c * 128:(c + 1) * 128],
                                         C.ident[:])
                return last
            P.ins("t", mk_trb, reads=[b_atok, C.b_ident], writes=[b_ps_tr])
            P.ins("v", lambda eng, qs=qs: eng.tensor_copy(
                      att[:, :, qs], ps_tr.ap().rearrange("p (c j) -> p c j", j=128)),
                  reads=[b_ps_tr], writes=[b_att])
            P.ins("v", lambda eng: eng.tensor_tensor(zs[:, :], zs[:, :], ps_u[0:64, :], ALU.add),
                  reads=[b_ps_u, b_zs], writes=[b_zs])
            P.ins("v", lambda eng: eng.tensor_tensor(zs[:, :], zs[:, :], w64b[0:64, :], ALU.mult),
                  reads=[b_w64b, b_zs], writes=[b_zs])
            P.ins("v", lambda eng: eng.tensor_tensor(zs[:, :], zs[:, :], w64b[0:64, :], ALU.mult),
                  reads=[b_w64b, b_zs], writes=[b_zs])
            def mk_zsb(eng):
                eng.tensor_copy(zsb[0:64, :], zs[:, :])
                return eng.tensor_copy(zsb[64:128, :], zs[:, :])
            P.ins("v", mk_zsb, reads=[b_zs], writes=[b_zsb])
        for c in range(NCH):
            P.ins("v", lambda eng, c=c: eng.tensor_tensor(att[:, c, :], att[:, c, :], gt[:, c, :],
                                                          ALU.mult),
                  reads=[b_gt, b_att], writes=[b_att])
        for f in range(NCH):
            wt, wb = wload(hb["a0_W_o"], f)
            ps, bps = ps_mm[f % 4]
            def mko(eng, wt=wt, ps=ps):
                last = None
                for k in range(NCH):
                    last = eng.matmul(ps[:, :], lhsT=wt[:, k, :], rhs=att[:, k, :],
                                      start=(k == 0), stop=(k == NCH - 1))
                return last
            P.ins("t", mko, reads=[wb, b_att], writes=[bps])
            P.ins("s", lambda eng, f=f, ps=ps, base=base: eng.activation(
                      y[:, f, base:base + 512], ps[:, :], AF.Copy),
                  reads=[bps], writes=[b_y[f]])

    P.barrier()
    if skip_fix:
        emit_ln(C, P, lnws, lambda c: y[:, c, 1:T + 1], b_y, None, None,
                ps_mm[2:4], src_bf16=True,
                gamma=lambda c: lng[:, c:c + 1], beta=lambda c: lnb[:, c:c + 1], b_gb=b_lng,
                accum_ap=lambda c: xr[:, c, :], accum_bufs=b_xr)
        return dict(y=y, b_y=b_y, att=att, b_att=b_att, gt=gt, b_gt=b_gt,
                    rkv=rkv, b_rkv=b_rkv, zs=zs, b_zs=b_zs)
    P.ins("v", lambda eng: eng.memset(agst[:], 0.0), writes=[b_agst])
    P.ins("v", lambda eng: eng.tensor_copy(agst[0:64, :], zs[:, :]),
          reads=[b_zs], writes=[b_agst])
    ago_z, b_ago_z = emit_allgather(C, P, "a0z", agst[:, :], b_agst, [128, H * DH])
    P.ins("v", lambda eng: eng.memset(zac[0:64, :], 0.0), writes=[b_zac])
    ldz = dl.ap().bitcast(F32).rearrange("p c q -> p (c q)")[0:64, 0:H * DH]
    b_ldz = b_dl
    for j in range(NCORES):
        P.ins("y", lambda eng, j=j: eng.dma_start(out=ldz, in_=ago_z[j * 128:j * 128 + 64, :]),
              reads=[b_ago_z], writes=[b_ldz], sem=C.dH, inc=16)
        P.ins("v", lambda eng, j=j: eng.scalar_tensor_tensor(
                  zac[0:64, :], ldz, stf[0:64, j:j + 1], zac[0:64, :], ALU.mult, ALU.add),
              reads=[b_ldz, b_stf], writes=[b_zac])
    def mk_zsb2(eng):
        eng.tensor_copy(zsb[0:64, :], zac[0:64, :])
        return eng.tensor_copy(zsb[64:128, :], zac[0:64, :])
    P.ins("v", mk_zsb2, reads=[b_zac], writes=[b_zsb])
    if skip_fix == 2:
        emit_ln(C, P, lnws, lambda c: y[:, c, 1:T + 1], b_y, None, None,
                ps_mm[2:4], src_bf16=True,
                gamma=lambda c: lng[:, c:c + 1], beta=lambda c: lnb[:, c:c + 1], b_gb=b_lng,
                accum_ap=lambda c: xr[:, c, :], accum_bufs=b_xr)
        return dict(y=y, b_y=b_y, att=att, b_att=b_att, gt=gt, b_gt=b_gt,
                    rkv=rkv, b_rkv=b_rkv, zs=zs, b_zs=b_zs)
    for h in range(H):
        c8, hb64 = h // 2, 64 * (h % 2)
        pso, bpso = ps_mm[2 + h // 8]
        off = (h % 8) * 64
        P.ins("t", lambda eng, h=h, c8=c8, hb64=hb64, pso=pso, off=off: eng.matmul(
                  pso[:, off:off + 64], lhsT=r0[hb64:hb64 + 64, c8, :],
                  rhs=zsb[hb64:hb64 + 64, h * 64:(h + 1) * 64], start=True, stop=True),
              reads=[b_r0, b_zsb], writes=[bpso])
    for h2 in range(2):
        pso, bpso = ps_mm[2 + h2]
        P.ins("v", lambda eng, h2=h2, pso=pso: eng.tensor_tensor(
                  atok[:, h2 * 512:(h2 + 1) * 512], pso[:, :],
                  wpos[:, h2 * 512:(h2 + 1) * 512], ALU.mult),
              reads=[bpso, b_wpos], writes=[b_atok])
    def mk_trf(eng):
        last = None
        for c in range(NCH):
            last = eng.transpose(ps_tr[:, c * 128:(c + 1) * 128], atok[:, c * 128:(c + 1) * 128],
                                 C.ident[:])
        return last
    P.ins("t", mk_trf, reads=[b_atok, C.b_ident], writes=[b_ps_tr])
    P.ins("v", lambda eng: eng.tensor_tensor(kt8.ap().rearrange("p c q -> p (c q)"), ps_tr[:, :],
                                             g0.ap().rearrange("p c q -> p (c q)"), ALU.mult),
          reads=[b_ps_tr, b_g0], writes=[b_kt8])
    if skip_fix != 3:
        for f in range(NCH):
            wt, wb = wload(hb["a0_W_o"], f)
            ps, bps = ps_mm[f % 4]
            def mkf(eng, wt=wt, ps=ps):
                last = None
                for k in range(NCH):
                    last = eng.matmul(ps[:, 0:CH], lhsT=wt[:, k, :], rhs=kt8[:, k, :],
                                      start=(k == 0), stop=(k == NCH - 1))
                return last
            P.ins("t", mkf, reads=[wb, b_kt8], writes=[bps])
            P.ins("v", lambda eng, f=f, ps=ps: eng.tensor_tensor(
                      y[:, f, 1:1 + CH], y[:, f, 1:1 + CH], ps[:, 0:CH], ALU.add),
                  reads=[bps], writes=[b_y[f]])

    emit_ln(C, P, lnws, lambda c: y[:, c, 1:T + 1], b_y, None, None,
            ps_mm[2:4], src_bf16=True,
            gamma=lambda c: lng[:, c:c + 1], beta=lambda c: lnb[:, c:c + 1], b_gb=b_lng,
            accum_ap=lambda c: xr[:, c, :], accum_bufs=b_xr)
    return dict(y=y, b_y=b_y, att=att, b_att=b_att, gt=gt, b_gt=b_gt,
                rkv=rkv, b_rkv=b_rkv, zs=zs, b_zs=b_zs)



def _np2dt(v):
    return {np.dtype(BF): BF16, np.dtype(np.float32): F32,
            np.dtype(np.int32): I32, np.dtype(np.uint32): U32}[v.dtype]


def build_a0_launch(shared_host, percore_sample, debug=False, skip_fix=False):
    import contextlib
    nc = bass.Bass("TRN2", target_bir_lowering=False,
                   detect_race_conditions=RACE_DETECT)
    P = Prog(nc)
    xin = nc.dram_tensor("xin", [128, NCH, T], F32, kind="ExternalInput").ap()
    xout = nc.dram_tensor("xout", [128, NCH, T], F32, kind="ExternalOutput").ap()
    hb = {}
    for src in (shared_host, percore_sample):
        for k, v in src.items():
            hb[k] = nc.dram_tensor(k, list(v.shape), _np2dt(v), kind="ExternalInput").ap()
    st = contextlib.ExitStack()
    with st:
        C = alloc_common(nc, st, P)
        lnws = alloc_lnws(C, st)
        xr, _ = _sb(C, st, "xres", [128, NCH, T], F32)
        b_xr = [Buf(f"xr{c}") for c in range(NCH)]
        def mk_ld(eng):
            return eng.dma_start(out=xr[:], in_=xin)
        P.ins("y", mk_ld, writes=b_xr, sem=C.dJ, inc=16)
        load_common(C, P, hb)
        dbg = emit_a0(C, P, st, hb, xr, b_xr, lnws, skip_fix=skip_fix)
        if debug:
            dy = nc.dram_tensor("dbg_y", [128, NCH, T + 1], BF16, kind="ExternalOutput").ap()
            datt = nc.dram_tensor("dbg_att", [128, NCH, 512], BF16, kind="ExternalOutput").ap()
            dgt = nc.dram_tensor("dbg_gt", [128, NCH, 512], BF16, kind="ExternalOutput").ap()
            drkv = nc.dram_tensor("dbg_rkv", [128, 3, NCH, 512], BF16, kind="ExternalOutput").ap()
            P.dmas("y", [lambda eng: eng.dma_start(out=dy, in_=dbg["y"][:])],
                   reads=dbg["b_y"], sem=C.dY)
            P.dmas("y", [lambda eng: eng.dma_start(out=datt, in_=dbg["att"][:])],
                   reads=[dbg["b_att"]], sem=C.dY)
            P.dmas("y", [lambda eng: eng.dma_start(out=dgt, in_=dbg["gt"][:])],
                   reads=[dbg["b_gt"]], sem=C.dY)
            P.dmas("y", [lambda eng: eng.dma_start(out=drkv, in_=dbg["rkv"][:])],
                   reads=dbg["b_rkv"], sem=C.dY)
        def mk_st(eng):
            return eng.dma_start(out=xout, in_=xr[:])
        tok = P.ins("y", mk_st, reads=b_xr, sem=C.dY, inc=16)
        P.wait("y", *tok)
        with nc.Block() as block:
            P.replay(block)
    return nc


def kernel(**inputs):
    raise NotImplementedError("assembled later")



H0 = 544
AGM_MODE = 0
RACE_DETECT = True
KVN = 2560
YC = H0 + T


def f0_host_shared(inp):
    d = {}
    wqkv = np.asarray(inp["L0_ffn_wqkv"], np.float32)
    wo = np.asarray(inp["L0_ffn_wo"], np.float32)
    w1 = np.asarray(inp["L0_ffn_w1"], np.float32)
    w2 = np.asarray(inp["L0_ffn_w2"], np.float32)
    b1 = np.asarray(inp["L0_ffn_b1"], np.float32)
    b2 = np.asarray(inp["L0_ffn_b2"], np.float32)
    HID = 2730
    d["f0_wqkv"] = np.stack([_tiles_fm(wqkv[l]) for l in range(2)])
    d["f0_wo"] = np.stack([_tiles_fm(wo[l]) for l in range(2)])
    d["f0_w1a"] = np.stack([_tiles_fm(w1[l][:, :HID]) for l in range(2)])
    d["f0_w1g"] = np.stack([_tiles_fm(w1[l][:, HID:]) for l in range(2)])
    d["f0_w2"] = np.stack([_tiles_fm(w2[l]) for l in range(2)])
    pb = np.zeros((2, 1, 2816), np.float32)
    pg = np.zeros((2, 1, 2816), np.float32)
    pb[:, 0, :HID] = b1[:, :HID]
    pg[:, 0, :HID] = b1[:, HID:]
    d["f0_b1a"] = pb
    d["f0_b1g"] = pg
    d["f0_b2"] = b2.reshape(2, 1, 1024)
    d["f0_lng"] = np.stack([_fmcol(np.asarray(inp["L0_ffn_ln_g"], np.float32)[l]) for l in range(2)])
    d["f0_lnb"] = np.stack([_fmcol(np.asarray(inp["L0_ffn_ln_b"], np.float32)[l]) for l in range(2)])
    qs = np.asarray(inp["L0_ffn_q_scale"], np.float32)
    ks = np.asarray(inp["L0_ffn_k_scale"], np.float32)
    d["f0_qs"] = np.ascontiguousarray(np.tile(qs, (1, 2)).reshape(2, 128).T)
    d["f0_ks"] = np.ascontiguousarray(np.tile(ks, (1, 2)).reshape(2, 128).T)
    qi = (np.arange(4)[:, None, None] * 128 + np.arange(128)[None, :, None])
    kj = np.arange(1024)[None, None, :]
    m = np.where((kj >= qi) & (kj <= qi + 512), 0.0, -1e9).astype(np.float32)
    d["f0_mask"] = np.ascontiguousarray(m.transpose(1, 0, 2)).astype(BF)
    R = np.zeros((64, 64), np.float32)
    for i in range(32):
        R[i, i + 32] = -1.0
        R[i + 32, i] = 1.0
    RB = np.zeros((128, 128), np.float32)
    RB[:64, :64] = R
    RB[64:, 64:] = R
    d["f0_rot"] = np.ascontiguousarray(RB.T).astype(BF)
    return d


def _f0_rotary_tables(t0):
    DIM_HEAD, WINDOW, N = 64, 512, 16384
    half_idx = np.arange(0, DIM_HEAD, 2, dtype=np.float64)
    inv_freq = 1.0 / 10000.0 ** (half_idx / DIM_HEAD)
    base = (half_idx + 0.4 * DIM_HEAD) / (1.4 * DIM_HEAD)

    def mk(n):
        t = np.arange(t0, t0 + n, dtype=np.float64)
        tc = np.clip(t, 0, None)
        freqs = tc[:, None] * inv_freq[None, :]
        power = (tc - N // 2) / float(WINDOW // 2)
        scale = base[None, :] ** power[:, None]
        cos, sin = np.cos(freqs), np.sin(freqs)
        cs = np.concatenate([cos * scale, cos * scale], 1)
        ss = np.concatenate([sin * scale, sin * scale], 1)
        ci = np.concatenate([cos / scale, cos / scale], 1)
        si = np.concatenate([sin / scale, sin / scale], 1)
        def fm(a):
            return np.ascontiguousarray(np.tile(a.T, (2, 1))).astype(np.float32).astype(BF)
        return fm(cs), fm(ss), fm(ci), fm(si)
    return mk


def f0_host_percore(core, toff=0):
    d = {}
    s = core * T + toff
    tq = _f0_rotary_tables(s)(T)
    tk = _f0_rotary_tables(s - 512)(KVN)
    d["f0_tq1"], d["f0_tq2"] = tq[0], tq[1]
    d["f0_tk1"], d["f0_tk2"] = tk[2], tk[3]
    d["f0_w0mask"] = np.full((128, 512), -1e9 if core == 0 else 0.0, np.float32).astype(BF)
    w = np.zeros((128, 8), dtype=np.float32)
    if core > 0:
        w[:, core - 1] = 1.0
    d["pc_stflag"] = w
    return d


def emit_f0(C, P, st, hb, xin_ap, b_xin, xacc_in_ap, xacc_out_ap, b_xacc, lnws,
            stop_after=None):
    nc = C.nc
    NW = 4

    ybuf, _ = _sb(C, st, "f0_y", [128, NCH, YC], BF16)
    b_y = [Buf(f"f0_y{c}") for c in range(NCH)]
    b_yh = Buf("f0_yhalo")
    kbuf, _ = _sb(C, st, "f0_k", [128, 4, KVN], BF16)
    b_k = Buf("f0_k")
    vbuf, _ = _sb(C, st, "f0_v", [128, 4, KVN], BF16)
    b_v = Buf("f0_v")
    qwin, b_q = _sb(C, st, "f0_q", [128, 4, 512], BF16)
    vtm, b_vtm = _sb(C, st, "f0_vtm", [128, 2, 4, 4, 128], BF16)
    rotk, b_rotk = _sb(C, st, "f0_rotk", [128, 2, 512], BF16)
    rotq, b_rotq = _sb(C, st, "f0_rotq", [128, 2, 512], BF16)
    msk, b_msk = _sb(C, st, "f0_msk", [128, 4, 1024], BF16)
    w0m, b_w0m = _sb(C, st, "f0_w0m", [128, 512], BF16)
    rotm, b_rotm = _sb(C, st, "f0_rotm", [128, 128], BF16)
    psb, b_psb = _sb(C, st, "f0_psb", [128, 1024], BF16)
    psbf, b_psbf = _sb(C, st, "f0_psbf", [128, 1024], F32)
    ptb, b_ptb = _sb(C, st, "f0_ptb", [128, 1024], BF16)
    ow, b_ow = _sb(C, st, "f0_ow", [128, 4, 512], BF16)
    h4, b_h4 = _sb(C, st, "f0_h4", [128, 22, 256], BF16)
    h2r, b_h2r = _sb(C, st, "f0_h2r", [128, NCH, 256], BF16)
    h2n, b_h2n = _sb(C, st, "f0_h2n", [128, NCH, 256], BF16)
    ycb2, b_ycb2 = _sb(C, st, "f0_ycb2", [128, NCH, NW * 2], BF16)
    xstr, _ = _sb(C, st, "f0_xstr", [128, 1, NCH, 512], F32)
    b_xstr = [Buf("f0_xstr0"), Buf("f0_xstr0b")]
    ldh, b_ldh = _sb(C, st, "f0_ldh", [128, NCH * H0], BF16)
    scol, b_scol = _sb(C, st, "f0_scol", [128, 8], F32)
    stf, b_stf = _sb(C, st, "f0_stf", [128, 8], F32)
    lngb, b_lngb = _sb(C, st, "f0_lngb", [128, 2, NCH], F32)
    lnbb, _ = _sb(C, st, "f0_lnbb", [128, 2, NCH], F32)
    qsc, b_qsc = _sb(C, st, "f0_qsc", [128, 2], F32)
    ksc, _ = _sb(C, st, "f0_ksc", [128, 2], F32)
    b1a, b_bias = _sb(C, st, "f0_b1a", [1, 2816], BF16)
    b1g, _ = _sb(C, st, "f0_b1g", [1, 2816], BF16)
    b2r, _ = _sb(C, st, "f0_b2", [1, 1024], BF16)

    psS_a, b_psS_a = _ps(C, st, "f0_Sa", [128, 1024])
    psS_b, b_psS_b = _ps(C, st, "f0_Sb", [128, 1024])
    ps_tr, b_ps_tr = _ps(C, st, "f0_tr", [128, 1024], BF16)
    psO, b_psO = _ps(C, st, "f0_O", [128, 512])
    proj_ps = [(psS_a[:, 0:512], b_psS_a), (psS_b[:, 0:512], b_psS_b),
               (psS_a[:, 512:1024], b_psS_a), (psS_b[:, 512:1024], b_psS_b)]
    pbc = [(psS_a[:, 0:512], b_psS_a), (psS_a[:, 512:1024], b_psS_a)]

    wq8 = [_sb(C, st, f"f0_wq8_{i}", [128, 8, 128], BF16) for i in range(2)]
    wq22 = [_sb(C, st, f"f0_wq22_{i}", [128, 22, 128], BF16) for i in range(2)]
    wslot = [0, 0]

    def wload8(mat_ap, f, kb=8):
        i = wslot[0]; wslot[0] ^= 1
        wt, wb = wq8[i]
        P.ins("g", lambda eng, f=f, kb=kb: eng.dma_start(
                  out=wt[:, 0:kb, :], in_=mat_ap[f].rearrange("k p f -> p k f")),
              writes=[wb], sem=(C.dA if i == 0 else C.dB), inc=16)
        return wt, wb

    def wload22(mat_ap, f):
        i = wslot[1]; wslot[1] ^= 1
        wt, wb = wq22[i]
        P.ins("g", lambda eng, f=f: eng.dma_start(
                  out=wt[:, :, :], in_=mat_ap[f].rearrange("k p f -> p k f")),
              writes=[wb], sem=(C.dC if i == 0 else C.dD), inc=16)
        return wt, wb

    P.dmas("y", [
        lambda eng: eng.dma_start(out=msk[:], in_=hb["f0_mask"]),
        lambda eng: eng.dma_start(out=w0m[:], in_=hb["f0_w0mask"]),
        lambda eng: eng.dma_start(out=rotm[:], in_=hb["f0_rot"]),
        lambda eng: eng.dma_start(out=stf[:], in_=hb["pc_stflag"]),
        lambda eng: eng.dma_start(out=lngb[:], in_=hb["f0_lng"].rearrange("l p c -> p l c")),
        lambda eng: eng.dma_start(out=lnbb[:], in_=hb["f0_lnb"].rearrange("l p c -> p l c")),
        lambda eng: eng.dma_start(out=qsc[:], in_=hb["f0_qs"]),
        lambda eng: eng.dma_start(out=ksc[:], in_=hb["f0_ks"]),
    ], writes=[b_msk, b_w0m, b_rotm, b_stf, b_lngb, b_qsc], sem=C.dL)
    P.barrier()

    def pre_tile(tt, sl, w):
        P.ins("y", lambda eng, sl=sl: eng.dma_start(
                  out=xstr[:, 0, :, :], in_=xin_ap[:, :, sl]),
              reads=[b_xin], writes=[b_xstr[0]], sem=C.dJ, inc=16)
        return (lambda c: xstr[:, 0, c, :]), [b_xstr[0]] * NCH
    emit_ln(C, P, lnws, None, None,
            lambda c: ybuf[:, c, H0:YC], b_y, pbc, n=T, pre_tile=pre_tile)

    dbg = dict(ybuf=ybuf, b_y=b_y, b_yh=b_yh, kbuf=kbuf, b_k=b_k, vbuf=vbuf, b_v=b_v,
               qwin=qwin, b_q=b_q, ow=ow, b_ow=b_ow)
    if stop_after == "ln":
        return dbg
    for l in range(2):
        P.dmas("g", [
            lambda eng, l=l: eng.dma_start(out=b1a[:], in_=hb["f0_b1a"][l]),
            lambda eng, l=l: eng.dma_start(out=b1g[:], in_=hb["f0_b1g"][l]),
            lambda eng, l=l: eng.dma_start(out=b2r[:], in_=hb["f0_b2"][l]),
        ], writes=[b_bias], sem=C.dE)
        P.barrier()
        src = ybuf.ap()[:, :, YC - H0:YC]
        ago, b_ago = emit_allgather(C, P, f"f0h{l}_{P.nins}", src, b_y[0], [128, NCH * H0],
                                    dt=BF16, b_extra=b_y)
        halo = ybuf.ap()[:, :, 0:H0]
        ldh3 = ldh.ap().rearrange("p (c t) -> p c t", c=NCH)
        P.ins("v", lambda eng, halo=halo: eng.memset(halo, 0.0), writes=[b_yh])
        for j in range(NCORES):
            P.ins("y", lambda eng, j=j: eng.dma_start(out=ldh[:], in_=ago[j * 128:(j + 1) * 128, :]),
                  reads=[b_ago], writes=[b_ldh], sem=C.dF, inc=16)
            P.ins("v", lambda eng, j=j, halo=halo, ldh3=ldh3: eng.scalar_tensor_tensor(
                      halo, ldh3, stf[:, j:j + 1], halo, ALU.mult, ALU.add),
                  reads=[b_ldh, b_stf], writes=[b_yh])
        P.barrier()

        for ti in (1, 2, 3, 4, 0):
            ycol = 32 + 512 * ti
            for f in range(4, 12):
                wt, wb = wload8(hb["f0_wqkv"][l], f)
                ps, bps = proj_ps[(ti * 8 + f) % 4]
                def mkp(eng, wt=wt, ps=ps, ycol=ycol):
                    last = None
                    for c in range(NCH):
                        off = ycol if c < 4 else ycol - 1
                        last = eng.matmul(ps[:, :], lhsT=wt[:, c, :],
                                          rhs=ybuf[:, c, off:off + 512],
                                          start=(c == 0), stop=(c == NCH - 1))
                    return last
                P.ins("t", mkp, reads=[wb, b_yh] + b_y, writes=[bps])
                dst, bdst = (kbuf, b_k) if f < 8 else (vbuf, b_v)
                P.ins("s", lambda eng, f=f, ps=ps, ti=ti, dst=dst: eng.activation(
                          dst[:, f % 4, 512 * ti:512 * ti + 512], ps[:, :], AF.Copy),
                      reads=[bps], writes=[bdst])
                P.fence("s", ps[0:1, 0:1], bps, C.fscr[0:1, 1:2])
        P.barrier()
        if stop_after == f"kv{l}":
            return dbg
        for ti in range(5):
            kcl = slice(512 * ti, 512 * ti + 512)
            P.dmas("y", [
                lambda eng, kcl=kcl: eng.dma_start(out=rotk[:, 0, :], in_=hb["f0_tk1"][:, kcl]),
                lambda eng, kcl=kcl: eng.dma_start(out=rotk[:, 1, :], in_=hb["f0_tk2"][:, kcl]),
            ], writes=[b_rotk], sem=C.dH)
            for kc in range(4):
                _f0_norm_rot(C, P, lnws, kbuf, b_k, kc, kcl, ksc[:, l:l + 1], b_qsc,
                             rotk, b_rotk, rotm, b_rotm, psS_a, b_psS_a, psS_b, b_psS_b,
                             psO, b_psO, scol, b_scol)
        P.barrier()
        if stop_after == f"krot{l}":
            return dbg
        _f0_vtm(C, P, vbuf, b_v, vtm, b_vtm, ps_tr, b_ps_tr, C.ident, C.b_ident, block=0)

        for w in range(NW):
            qcol = H0 + 512 * w
            for f in range(4):
                wt, wb = wload8(hb["f0_wqkv"][l], f)
                ps, bps = proj_ps[f % 4]
                def mkq(eng, wt=wt, ps=ps, qcol=qcol):
                    last = None
                    for c in range(NCH):
                        off = qcol if c < 4 else qcol - 1
                        last = eng.matmul(ps[:, :], lhsT=wt[:, c, :],
                                          rhs=ybuf[:, c, off:off + 512],
                                          start=(c == 0), stop=(c == NCH - 1))
                    return last
                P.ins("t", mkq, reads=[wb, b_yh] + b_y, writes=[bps])
                P.ins("s", lambda eng, f=f, ps=ps: eng.activation(qwin[:, f, :], ps[:, :], AF.Copy),
                      reads=[bps], writes=[b_q])
                P.fence("s", ps[0:1, 0:1], bps, C.fscr[0:1, 1:2])
            P.dmas("y", [
                lambda eng, w=w: eng.dma_start(out=rotq[:, 0, :], in_=hb["f0_tq1"][:, 512 * w:512 * w + 512]),
                lambda eng, w=w: eng.dma_start(out=rotq[:, 1, :], in_=hb["f0_tq2"][:, 512 * w:512 * w + 512]),
            ], writes=[b_rotq], sem=C.dI)
            for qc in range(4):
                _f0_norm_rot(C, P, lnws, qwin, b_q, qc, slice(0, 512), qsc[:, l:l + 1], b_qsc,
                             rotq, b_rotq, rotm, b_rotm, psS_a, b_psS_a, psS_b, b_psS_b,
                             psO, b_psO, scol, b_scol)
            P.barrier()
            _f0_vtm(C, P, vbuf, b_v, vtm, b_vtm, ps_tr, b_ps_tr, C.ident, C.b_ident, block=w + 1)
            for h in range(8):
                hc, hb64 = h // 2, 64 * (h % 2)
                for qt in range(4):
                    psS, bS = (psS_a, b_psS_a) if (h * 4 + qt) % 2 == 0 else (psS_b, b_psS_b)
                    def mkS(eng, psS=psS, hc=hc, hb64=hb64, qt=qt, w=w):
                        eng.matmul(psS[:, 0:512],
                                   lhsT=qwin[hb64:hb64 + 64, hc, qt * 128:qt * 128 + 128],
                                   rhs=kbuf[hb64:hb64 + 64, hc, 512 * w:512 * w + 512],
                                   start=True, stop=True)
                        return eng.matmul(psS[:, 512:1024],
                                          lhsT=qwin[hb64:hb64 + 64, hc, qt * 128:qt * 128 + 128],
                                          rhs=kbuf[hb64:hb64 + 64, hc, 512 * w + 512:512 * w + 1024],
                                          start=True, stop=True)
                    P.ins("t", mkS, reads=[b_q, b_k], writes=[bS])
                    def mkM(eng, psS=psS, qt=qt):
                        return eng.tensor_tensor(psS[:, :], psS[:, :], msk[:, qt, :], ALU.add)
                    P.ins("v", mkM, reads=[b_msk], writes=[bS])
                    if w == 0:
                        P.ins("v", lambda eng, psS=psS: eng.tensor_tensor(
                                  psS[:, 0:512], psS[:, 0:512], w0m[:, :], ALU.add),
                              reads=[b_w0m], writes=[bS])
                    P.ins("v", lambda eng, psS=psS: eng.tensor_reduce(
                              scol[:, 0:1], psS[:, :], AX.X, ALU.max),
                          reads=[bS], writes=[b_scol])
                    P.ins("v", lambda eng, psS=psS: eng.tensor_scalar(
                              psbf[:, :], psS[:, :], scol[:, 0:1], 8.0,
                              ALU.subtract, ALU.mult),
                          reads=[bS, b_scol], writes=[b_psbf])
                    P.fence("v", psS[0:1, 0:1], bS, C.fscr[0:1, 0:1])
                    P.ins("s", lambda eng: eng.activation(
                              psb[:, :], psbf[:, :], AF.Exp, accum_out=scol[:, 2:3]),
                          reads=[b_psbf], writes=[b_psb, b_scol])
                    P.ins("s", lambda eng: eng.activation(scol[:, 4:5], scol[:, 2:3], AF.Copy),
                          reads=[b_scol], writes=[b_scol])
                    P.ins("v", lambda eng: eng.reciprocal(scol[:, 3:4], scol[:, 2:3]),
                          reads=[b_scol], writes=[b_scol])
                    P.ins("v", lambda eng: eng.tensor_scalar(psb[:, :], psb[:, :],
                                                             scol[:, 3:4], None, ALU.mult),
                          reads=[b_scol], writes=[b_psb])
                    def mkT(eng):
                        last = None
                        for kb in range(8):
                            last = eng.transpose(ps_tr[:, kb * 128:(kb + 1) * 128],
                                                 psb[:, kb * 128:(kb + 1) * 128], C.ident[:])
                        return last
                    P.ins("t", mkT, reads=[b_psb, C.b_ident], writes=[b_ps_tr])
                    P.ins("v", lambda eng: eng.tensor_copy(ptb[:, :], ps_tr[:, :]),
                          reads=[b_ps_tr], writes=[b_ptb])
                    P.fence("v", ps_tr[0:1, 0:1], b_ps_tr, C.fscr[0:1, 0:1])
                    def mkPV(eng, h=h, hc=hc, hb64=hb64, w=w):
                        last = None
                        for kb in range(8):
                            blk = w + kb // 4
                            last = eng.matmul(psO[hb64:hb64 + 64, 0:128],
                                              lhsT=vtm[:, blk % 2, kb % 4, hc, hb64:hb64 + 64],
                                              rhs=ptb[:, kb * 128:(kb + 1) * 128],
                                              start=(kb == 0), stop=(kb == 7))
                        return last
                    P.ins("t", mkPV, reads=[b_ptb, b_vtm], writes=[b_psO])
                    P.ins("s", lambda eng, hc=hc, hb64=hb64, qt=qt: eng.activation(
                              ow[hb64:hb64 + 64, hc, qt * 128:qt * 128 + 128],
                              psO[hb64:hb64 + 64, 0:128], AF.Copy),
                          reads=[b_psO], writes=[b_ow])
                    P.fence("s", psO[0:1, 0:1], b_psO, C.fscr[0:1, 1:2])
            P.barrier()
            if stop_after == f"attn0{l}" and w == 0:
                return dbg
            if l == 1 and w == 0 and "snap3" not in dbg:
                snap3, b_snap3 = _sb(C, st, "f0_snap3", [128, NCH, 4], BF16)
                P.ins("v", lambda eng, snap3=snap3: eng.tensor_copy(
                          snap3[:, :, :], ybuf[:, :, H0:H0 + 4]),
                      reads=b_y, writes=[b_snap3])
                dbg["snap3"] = snap3
                dbg["b_snap3"] = b_snap3
            for f in range(NCH):
                wt, wb = wload8(hb["f0_wo"][l], f, kb=4)
                ps, bps = proj_ps[f % 4]
                def mkO(eng, wt=wt, ps=ps):
                    last = None
                    for c in range(4):
                        last = eng.matmul(ps[:, :], lhsT=wt[:, c, :], rhs=ow[:, c, :],
                                          start=(c == 0), stop=(c == 3))
                    return last
                P.ins("t", mkO, reads=[wb, b_ow], writes=[bps])
                P.ins("v", lambda eng, f=f, ps=ps, qcol=qcol: eng.tensor_tensor(
                          ybuf[:, f, qcol:qcol + 512], ybuf[:, f, qcol:qcol + 512],
                          ps[:, :], ALU.add),
                      reads=[bps], writes=[b_y[f]])
            P.barrier()
        P.barrier()
        if stop_after == f"ymid{l}":
            return dbg
        P.ins("v", lambda eng: eng.memset(ldh[:, 0:64], 0.0), writes=[b_ldh])
        P.ins("v", lambda eng: eng.tensor_copy(ldh[:, 0:NCH], ybuf[:, :, YC - 1:YC].squeeze(2)),
              reads=b_y, writes=[b_ldh])
        if AGM_MODE < 2:
            agm, b_agm = emit_allgather(C, P, f"f0m{l}_{P.nins}", ldh[:, 0:64], b_ldh, [128, 64],
                                        dt=BF16)
        P.ins("v", lambda eng: eng.memset(scol[:, 0:NCH], 0.0), writes=[b_scol])
        if AGM_MODE < 1:
            for j in range(NCORES):
                P.ins("y", lambda eng, j=j: eng.dma_start(out=ldh[:, 64:128],
                                                          in_=agm[j * 128:(j + 1) * 128, :]),
                      reads=[b_agm], writes=[b_ldh], sem=C.dF, inc=16)
                P.ins("v", lambda eng, j=j: eng.scalar_tensor_tensor(
                          scol[:, 0:NCH], ldh[:, 64:64 + NCH], stf[:, j:j + 1], scol[:, 0:NCH],
                          ALU.mult, ALU.add),
                      reads=[b_ldh, b_stf], writes=[b_scol])
        P.ins("v", lambda eng: eng.tensor_copy(ybuf[:, :, H0 - 1:H0].squeeze(2), scol[:, 0:NCH]),
              reads=[b_scol], writes=[b_yh])
        P.barrier()
        if l == 1:
            snap, b_snap = _sb(C, st, "f0_snap", [128, NCH, 4], BF16)
            def mksnap(eng):
                return eng.tensor_copy(snap[:, :, :], ybuf[:, :, H0 - 1:H0 + 3])
            P.ins("v", mksnap, reads=b_y + [b_yh], writes=[b_snap])
            dbg["snap"] = snap
            dbg["b_snap"] = b_snap
        if stop_after == f"agm{l}":
            return dbg
        def mk_ycb(eng):
            last = None
            for tt in range(8):
                for c in range(NCH):
                    col = H0 - 1 + 256 * tt
                    last = eng.tensor_copy(ycb2[:, c, tt:tt + 1], ybuf[:, c, col:col + 1])
            return last
        P.ins("v", mk_ycb, reads=b_y + [b_yh], writes=[b_ycb2])
        def ffn_pre(tt, sl, w):
            P.barrier()
            for c in range(4, NCH):
                def mkh(eng, c=c, tt=tt, w=w):
                    eng.tensor_copy(h2r[:, c, 0:1], ycb2[:, c, tt:tt + 1])
                    return eng.tensor_copy(h2r[:, c, 1:w],
                                           ybuf[:, c, H0 + 256 * tt:H0 + 256 * tt + w - 1])
                P.ins("v", mkh, reads=[b_ycb2, b_y[c]], writes=[b_h2r])
            def x_fn(c, tt=tt, w=w):
                if c < 4:
                    return ybuf[:, c, H0 + 256 * tt:H0 + 256 * tt + w]
                return h2r[:, c, 0:w]
            return x_fn, [b_y[c] if c < 4 else b_h2r for c in range(NCH)]
        def ffn_tile(tt, sl, w):
            for j in range(22):
                wta, wba = wload8(hb["f0_w1a"][l], j)
                ps_a, bps_a = proj_ps[j % 2]
                def mk1a(eng, wta=wta, ps_a=ps_a, w=w, j=j):
                    for c in range(NCH):
                        eng.matmul(ps_a[:, 0:w], lhsT=wta[:, c, :], rhs=h2n[:, c, 0:w],
                                   start=(c == 0), stop=False)
                    return eng.matmul(ps_a[:, 0:w], lhsT=b1a[0:1, j * 128:(j + 1) * 128],
                                      rhs=C.onesb[0:1, 0:w], start=False, stop=True)
                P.ins("t", mk1a, reads=[wba, b_h2n, b_bias, C.b_ones], writes=[bps_a])
                wtg, wbg = wload8(hb["f0_w1g"][l], j)
                ps_g, bps_g = proj_ps[2 + j % 2]
                def mk1g(eng, wtg=wtg, ps_g=ps_g, w=w, j=j):
                    for c in range(NCH):
                        eng.matmul(ps_g[:, 0:w], lhsT=wtg[:, c, :], rhs=h2n[:, c, 0:w],
                                   start=(c == 0), stop=False)
                    return eng.matmul(ps_g[:, 0:w], lhsT=b1g[0:1, j * 128:(j + 1) * 128],
                                      rhs=C.onesb[0:1, 0:w], start=False, stop=True)
                P.ins("t", mk1g, reads=[wbg, b_h2n, b_bias, C.b_ones], writes=[bps_g])
                P.ins("s", lambda eng, ps_g=ps_g, w=w: eng.activation(
                          psb[:, 0:w], ps_g[:, 0:w], AF.Gelu),
                      reads=[bps_g], writes=[b_psb])
                P.fence("s", ps_g[0:1, 0:1], bps_g, C.fscr[0:1, 1:2])
                P.ins("v", lambda eng, j=j, ps_a=ps_a, w=w: eng.tensor_tensor(
                          h4[:, j, 0:w], ps_a[:, 0:w], psb[:, 0:w], ALU.mult),
                      reads=[bps_a, b_psb], writes=[b_h4])
                P.fence("v", ps_a[0:1, 0:1], bps_a, C.fscr[0:1, 0:1])
            for f in range(NCH):
                wt2, wb2 = wload22(hb["f0_w2"][l], f)
                ps, bps = proj_ps[f % 4]
                def mk2(eng, wt2=wt2, ps=ps, w=w, f=f):
                    for j in range(22):
                        kw = 128 if j < 21 else 42
                        eng.matmul(ps[:, 0:w], lhsT=wt2[0:kw, j, :], rhs=h4[0:kw, j, 0:w],
                                   start=(j == 0), stop=False)
                    return eng.matmul(ps[:, 0:w], lhsT=b2r[0:1, f * 128:(f + 1) * 128],
                                      rhs=C.onesb[0:1, 0:w], start=False, stop=True)
                P.ins("t", mk2, reads=[wb2, b_h4, b_bias, C.b_ones], writes=[bps])
                P.ins("v", lambda eng, f=f, ps=ps, tt=tt, w=w: eng.tensor_tensor(
                          ybuf[:, f, H0 + 256 * tt:H0 + 256 * tt + w],
                          ybuf[:, f, H0 + 256 * tt:H0 + 256 * tt + w], ps[:, 0:w], ALU.add),
                      reads=[bps], writes=[b_y[f]])
                P.fence("v", ps[0:1, 0:1], bps, C.fscr[0:1, 0:1])
            P.barrier()
        ffn_lim = 1 if stop_after == f"ffn1t{l}" else None
        emit_ln(C, P, lnws, None, None, lambda c: h2n[:, c, :], [b_h2n] * NCH, pbc,
                n=T, tstep=256, src_bf16=True, pre_tile=ffn_pre, tile_cb=ffn_tile,
                gamma=lambda c, l=l: lngb[:, l, c:c + 1],
                beta=lambda c, l=l: lnbb[:, l, c:c + 1], b_gb=b_lngb, out_local=True,
                ntiles_limit=ffn_lim)
        if ffn_lim:
            dbg["h2n"] = h2n
            dbg["b_h2n"] = b_h2n
            dbg["h2r"] = h2r
            dbg["ycb2"] = ycb2
            dbg["b_ycb2"] = b_ycb2
            return dbg
        P.barrier()
        if stop_after == f"layer1{l}":
            return dbg
    for tt in range(4):
        sl = slice(512 * tt, 512 * tt + 512)
        P.ins("y", lambda eng, sl=sl: eng.dma_start(out=xstr[:, 0, :, :],
                                                    in_=xacc_in_ap[:, :, sl]),
              reads=[b_xacc], writes=[b_xstr[0]], sem=C.dJ, inc=16)
        for c in range(NCH):
            P.ins("v", lambda eng, c=c, sl=sl: eng.scalar_tensor_tensor(
                      xstr[:, 0, c, :], ybuf[:, c, H0 + sl.start:H0 + sl.start + 512], 0.5,
                      xstr[:, 0, c, :], ALU.mult, ALU.add),
                  reads=[b_y[c]], writes=[b_xstr[0]])
        P.ins("y", lambda eng, sl=sl: eng.dma_start(out=xacc_out_ap[:, :, sl],
                                                    in_=xstr[:, 0, :, :]),
              reads=[b_xstr[0]], writes=[b_xacc], sem=C.dJ, inc=16)
    P.barrier()
    return dbg


def _f0_norm_rot(C, P, lnws, buf, b_buf, cix, cols, sc_col, b_sc, rot, b_rot,
                 rotm, b_rotm, psS_a, b_psS_a, psS_b, b_psS_b, psO, b_psO, scol, b_scol):
    ps_st, b_ps_st = lnws["ps_st"], lnws["b_ps_st"]
    sq, b_sq = lnws["sq"], lnws["b_sq"]
    stxa, stxb, b_stx = lnws["stxa"], lnws["stxb"], lnws["b_stx"]
    v = buf[:, cix, cols]
    P.ins("s", lambda eng: eng.activation(sq[:, :], v, AF.Square),
          reads=[b_buf], writes=[b_sq])
    def mkst(eng):
        eng.matmul(ps_st[0:1, :], lhsT=C.bones[:, 0:1], rhs=sq[:, :], start=True, stop=True)
        return eng.matmul(ps_st[32:33, :], lhsT=C.bones[:, 1:2], rhs=sq[:, :],
                          start=True, stop=True)
    P.ins("t", mkst, reads=[b_sq, C.b_ones], writes=[b_ps_st])
    def mkn(eng):
        eng.activation(stxa[0:1, :], ps_st[0:1, :], AF.Sqrt)
        return eng.activation(stxb[0:1, :], ps_st[32:33, :], AF.Sqrt)
    P.ins("s", mkn, reads=[b_ps_st], writes=[b_stx])
    def mkc(eng):
        eng.tensor_scalar(stxa[0:1, :], stxa[0:1, :], 1e-12, None, ALU.max)
        eng.tensor_scalar(stxb[0:1, :], stxb[0:1, :], 1e-12, None, ALU.max)
        eng.reciprocal(stxa[0:1, :], stxa[0:1, :])
        return eng.reciprocal(stxb[0:1, :], stxb[0:1, :])
    P.ins("v", mkc, reads=[b_stx], writes=[b_stx])
    def mkb(eng):
        eng.matmul(psO[:, 0:512], lhsT=C.ones_row[0:1, :], rhs=stxa[0:1, :],
                   start=True, stop=True)
        return eng.matmul(psS_a[:, 0:512], lhsT=C.ones_row[0:1, :], rhs=stxb[0:1, :],
                          start=True, stop=True)
    P.ins("t", mkb, reads=[b_stx, C.b_ones], writes=[b_psO, b_psS_a])
    def mkapA(eng):
        return eng.scalar_tensor_tensor(v[0:64, :] if False else buf[0:64, cix, cols],
                                        buf[0:64, cix, cols], sc_col[0:64, :],
                                        psO[0:64, 0:512], ALU.mult, ALU.mult)
    def mkapB(eng):
        return eng.scalar_tensor_tensor(buf[64:128, cix, cols], buf[64:128, cix, cols],
                                        sc_col[64:128, :], psS_a[64:128, 0:512],
                                        ALU.mult, ALU.mult)
    P.ins("v", mkapA, reads=[b_psO, b_sc], writes=[b_buf])
    P.ins("v", mkapB, reads=[b_psS_a, b_sc], writes=[b_buf])
    P.ins("t", lambda eng: eng.matmul(psS_b[:, 0:512], lhsT=rotm[:, :], rhs=v,
                                      start=True, stop=True),
          reads=[b_rotm, b_buf], writes=[b_psS_b])
    P.ins("v", lambda eng: eng.tensor_tensor(sq[:, :], psS_b[:, 0:512], rot[:, 1, :], ALU.mult),
          reads=[b_psS_b, b_rot], writes=[b_sq])
    P.ins("v", lambda eng: eng.tensor_tensor(v, v, rot[:, 0, :], ALU.mult),
          reads=[b_rot], writes=[b_buf])
    P.ins("v", lambda eng: eng.tensor_tensor(v, v, sq[:, :], ALU.add),
          reads=[b_sq], writes=[b_buf])


def _f0_vtm(C, P, vbuf, b_v, vtm, b_vtm, ps_tr, b_ps_tr, ident, b_ident, block):
    for vc in range(4):
        def mkt(eng, vc=vc, block=block):
            last = None
            for j in range(4):
                last = eng.transpose(ps_tr[:, j * 128:(j + 1) * 128],
                                     vbuf[:, vc, 512 * block + 128 * j:512 * block + 128 * j + 128],
                                     ident[:])
            return last
        P.ins("t", mkt, reads=[b_v, b_ident], writes=[b_ps_tr])
        def mkc(eng, vc=vc, block=block):
            return eng.tensor_copy(vtm.ap()[:, block % 2, :, vc, :], 
                                   ps_tr.ap()[:, 0:512].rearrange("p (j f) -> p j f", j=4))
        P.ins("v", mkc, reads=[b_ps_tr], writes=[b_vtm])


def build_f0_launch(shared_host, percore_sample, debug=False, stop_after=None):
    import contextlib
    nc = bass.Bass("TRN2", target_bir_lowering=False,
                   detect_race_conditions=RACE_DETECT)
    P = Prog(nc)
    xin = nc.dram_tensor("xin", [128, NCH, T], F32, kind="ExternalInput").ap()
    xout = nc.dram_tensor("xout", [128, NCH, T], F32, kind="ExternalOutput").ap()
    hb = {}
    for srcd in (shared_host, percore_sample):
        for k, v in srcd.items():
            hb[k] = nc.dram_tensor(k, list(v.shape), _np2dt(v), kind="ExternalInput").ap()
    st = contextlib.ExitStack()
    with st:
        C = alloc_common(nc, st, P)
        lnws = alloc_lnws(C, st)
        load_common(C, P, hb)
        b_xin = Buf("xin")
        b_xacc = Buf("xacc")
        dbg = emit_f0(C, P, st, hb, xin, b_xin, xin, xout, b_xacc, lnws,
                      stop_after=stop_after)
        if debug:
            P.barrier()
            dn = {}
            dn["dbg_y"] = (nc.dram_tensor("dbg_y", [128, NCH, YC], BF16,
                                          kind="ExternalOutput").ap(), dbg["ybuf"], dbg["b_y"])
            dn["dbg_k"] = (nc.dram_tensor("dbg_k", [128, 4, KVN], BF16,
                                          kind="ExternalOutput").ap(), dbg["kbuf"], [dbg["b_k"]])
            dn["dbg_v"] = (nc.dram_tensor("dbg_v", [128, 4, KVN], BF16,
                                          kind="ExternalOutput").ap(), dbg["vbuf"], [dbg["b_v"]])
            dn["dbg_q"] = (nc.dram_tensor("dbg_q", [128, 4, 512], BF16,
                                          kind="ExternalOutput").ap(), dbg["qwin"], [dbg["b_q"]])
            dn["dbg_ow"] = (nc.dram_tensor("dbg_ow", [128, 4, 512], BF16,
                                           kind="ExternalOutput").ap(), dbg["ow"], [dbg["b_ow"]])
            if "snap" in dbg:
                dn["dbg_snap"] = (nc.dram_tensor("dbg_snap", [128, NCH, 4], BF16,
                                                 kind="ExternalOutput").ap(), dbg["snap"],
                                  [dbg["b_snap"]])
            if "snap3" in dbg:
                dn["dbg_snap3"] = (nc.dram_tensor("dbg_snap3", [128, NCH, 4], BF16,
                                                  kind="ExternalOutput").ap(), dbg["snap3"],
                                   [dbg["b_snap3"]])
            if "snap2" in dbg:
                dn["dbg_snap2"] = (nc.dram_tensor("dbg_snap2", [128, NCH, 4], BF16,
                                                  kind="ExternalOutput").ap(), dbg["snap2"],
                                   [dbg["b_snap2"]])
            if "h2n" in dbg:
                dn["dbg_h2n"] = (nc.dram_tensor("dbg_h2n", [128, NCH, 256], BF16,
                                                kind="ExternalOutput").ap(), dbg["h2n"],
                                 [dbg["b_h2n"]])
                dn["dbg_h2r"] = (nc.dram_tensor("dbg_h2r", [128, NCH, 256], BF16,
                                                kind="ExternalOutput").ap(), dbg["h2r"], [])
                dn["dbg_ycb2"] = (nc.dram_tensor("dbg_ycb2", [128, NCH, 8], BF16,
                                                 kind="ExternalOutput").ap(), dbg["ycb2"],
                                  [dbg["b_ycb2"]])
            for nm, (ap, tns, bufs) in dn.items():
                P.dmas("y", [lambda eng, ap=ap, tns=tns: eng.dma_start(out=ap, in_=tns[:])],
                       reads=list(bufs), sem=C.dY)
        P.wait("y", C.dY, C.dY.v)
        with nc.Block() as block:
            P.replay(block)
    return nc



def f1_host_shared(inp):
    d = {}
    d["f1_wq"] = _tiles_fm(np.asarray(inp["L1_ffn_Wq"], np.float32).T)
    keys = np.asarray(inp["L1_ffn_keys"], np.float32)
    kk = keys.transpose(2, 0, 3, 1)
    kk2 = np.concatenate([kk, kk], axis=2)
    d["f1_keys"] = np.ascontiguousarray(kk2.transpose(2, 0, 1, 3))
    down = np.asarray(inp["L1_ffn_down_embed"], np.float32)
    up = np.asarray(inp["L1_ffn_up_embed"], np.float32)
    d["f1_emb"] = np.ascontiguousarray(np.concatenate([down, up], axis=1))
    d["f1_wg"] = _tiles_fm(np.asarray(inp["L1_ffn_Wg"], np.float32).T)
    d["f1_wu"] = _tiles_fm(np.asarray(inp["L1_ffn_Wu"], np.float32).T)
    d["f1_wd"] = _tiles_fm(np.asarray(inp["L1_ffn_Wd"], np.float32).T)
    iot = np.tile(np.arange(8, dtype=np.float32)[None, :], (128, 1))
    d["f1_iota8"] = iot
    return d


def emit_f1(C, P, st, hb, xin_ap, b_xin, xacc_in_ap, xacc_out_ap, b_xacc, lnws,
            parts="both", dbg_mlp=False):
    nc = C.nc
    NT16 = T // 128

    yb, _ = _sb(C, st, "f1_yb", [128, NCH, T], BF16)
    b_yb = [Buf(f"f1_yb{c}") for c in range(NCH)]
    yf, _ = _sb(C, st, "f1_yf", [128, NCH, T], F32)
    b_yf = [Buf(f"f1_yf{c}") for c in range(NCH)]
    xstr, _ = _sb(C, st, "f1_xstr", [128, NCH, 256], F32)
    b_xstr = Buf("f1_xstr")
    qt_s, b_qf = _sb(C, st, "f1_q", [128, 4, 128], F32)
    keys, b_keys = _sb(C, st, "f1_keys", [128, 2, 4, 128], F32)
    iota8, b_iota8 = _sb(C, st, "f1_iota8", [128, 8], F32)
    ytm, b_ytm = _sb(C, st, "f1_ytm", [128, 1024], BF16)
    gth = [_sb(C, st, f"f1_gth{i}", [128, 2048], F32) for i in range(2)]
    acct, b_acct = _sb(C, st, "f1_acct", [128, 1024], F32)
    junk, b_junk = _sb(C, st, "f1_junk", [128, 1024], BF16)
    tmpf, b_tmpf = junk.ap().bitcast(F32)[:, 0:512], b_junk
    sx, b_sx = _sb(C, st, "f1_sx", [128, 2, 4, 8], F32)
    ixf, b_ixf = _sb(C, st, "f1_ix", [128, 2, 4, 8], F32)
    ixu, b_ixu = _sb(C, st, "f1_ixu", [128, 8], U32)
    comb, b_comb = _sb(C, st, "f1_comb", [128, 64], F32)
    sc8, b_sc8 = _sb(C, st, "f1_sc8", [128, 4, 8], F32)
    pku, b_pku = _sb(C, st, "f1_pku", [128, 8], U32)
    pka, b_pka = _sb(C, st, "f1_pka", [128, 2, 8], F32)
    oh, b_oh = _sb(C, st, "f1_oh", [128, 64], F32)
    idxs, b_idxs = _sb(C, st, "f1_idxs", [128, 4, 8], I32)
    wcol, b_wcol = _sb(C, st, "f1_wcol", [128, 4, 8], F32)
    scs, b_scs = _sb(C, st, "f1_scs", [128, 8], F32)

    ps_mlp, b_ps_mlp = _ps(C, st, "f1_mlp", [128, 8, 256])
    pbc = [(ps_mlp.ap()[:, 0:2, :].rearrange("p a b -> p (a b)"), b_ps_mlp),
           (ps_mlp.ap()[:, 2:4, :].rearrange("p a b -> p (a b)"), b_ps_mlp)]
    ps_gu, b_ps_gu = _ps(C, st, "f1_gu", [128, 512])
    ps_sim, b_ps_sim = _ps(C, st, "f1_sim", [128, 512])
    ps_trb, b_ps_trb = _ps(C, st, "f1_trb", [128, 1024], BF16)
    wgq = [_sb(C, st, f"f1_wg{j}", [128, 8, 128], BF16) for j in range(8)]
    wuq = [_sb(C, st, f"f1_wu{j}", [128, 8, 128], BF16) for j in range(8)]
    wdj = [_sb(C, st, f"f1_wdj{i}", [128, 128], BF16) for i in range(2)]
    hbuf, _ = _sb(C, st, "f1_hbuf", [128, 2048], BF16)
    b_hb = Buf("f1_hbuf")
    wqf = [_sb(C, st, f"f1_wq{i}", [128, 8, 128], F32) for i in range(4)]

    P.dmas("y", [
        lambda eng: eng.dma_start(out=keys[:], in_=hb["f1_keys"]),
        lambda eng: eng.dma_start(out=iota8[:], in_=hb["f1_iota8"]),
    ], writes=[b_keys, b_iota8], sem=C.dL)
    P.barrier()

    def pre_tile(tt, sl, w):
        P.ins("y", lambda eng, sl=sl: eng.dma_start(out=xstr[:, :, :], in_=xin_ap[:, :, sl]),
              reads=[b_xin], writes=[b_xstr], sem=C.dJ, inc=16)
        return (lambda c: xstr[:, c, :]), [b_xstr] * NCH
    emit_ln(C, P, lnws, None, None, lambda c: yf[:, c, :], b_yf, pbc, n=T,
            pre_tile=pre_tile, tstep=256)
    for c in range(NCH):
        for tt in range(4):
            sl = slice(tt * 512, tt * 512 + 512)
            P.ins("v", lambda eng, c=c, sl=sl: eng.tensor_copy(yb[:, c, sl], yf[:, c, sl]),
                  reads=[b_yf[c]], writes=[b_yb[c]])
    P.barrier()

    for f in range(4):
        wt, wb = wqf[f]
        P.ins("y", lambda eng, f=f, wt=wt: eng.dma_start(
                  out=wt[:, :, :], in_=hb["f1_wq"][f].rearrange("k p f -> p k f")),
              writes=[wb], sem=C.dY, inc=16)
    P.barrier()

    for tt in range(NT16):
        t0 = tt * 128
        for f in range(4):
            wt, wb = wqf[f]
            def mkq(eng, wt=wt, t0=t0):
                last = None
                for k in range(NCH):
                    last = eng.matmul(ps_sim[:, 256:384], lhsT=wt[:, k, :],
                                      rhs=yf[:, k, t0:t0 + 128],
                                      start=(k == 0), stop=(k == NCH - 1))
                return last
            P.ins("t", mkq, reads=[wb] + b_yf, writes=[b_ps_sim])
            P.ins("v", lambda eng, f=f: eng.tensor_copy(qt_s[:, f, :], ps_sim[:, 256:384]),
                  reads=[b_ps_sim], writes=[b_qf])
            P.fence("v", ps_sim[0:1, 256:257], b_ps_sim, C.fscr[0:1, 0:1])
        for p in range(2):
            for h in range(4):
                qc = p * 2 + h // 2
                qb = 64 * (h % 2)
                psl = ps_sim[:, 0:128] if (p * 4 + h) % 2 == 0 else ps_sim[:, 128:256]
                P.ins("t", lambda eng, qc=qc, qb=qb, p=p, h=h, psl=psl: eng.matmul(
                          psl, lhsT=qt_s[qb:qb + 64, qc, :],
                          rhs=keys[qb:qb + 64, p, h, :], start=True, stop=True),
                      reads=[b_qf, b_keys], writes=[b_ps_sim])
                P.ins("v", lambda eng, p=p, h=h, psl=psl: eng.max(sx[:, p, h, :], psl),
                      reads=[b_ps_sim], writes=[b_sx])
                P.ins("v", lambda eng, p=p, h=h, psl=psl: eng.max_index(ixu[:, :], sx[:, p, h, :], psl),
                      reads=[b_ps_sim, b_sx], writes=[b_ixu])
                P.ins("v", lambda eng, p=p, h=h: eng.tensor_copy(ixf[:, p, h, :], ixu[:, :]),
                      reads=[b_ixu], writes=[b_ixf])
                P.fence("v", psl[0:1, 0:1], b_ps_sim, C.fscr[0:1, 0:1])
        for h in range(4):
            P.ins("v", lambda eng, h=h: eng.tensor_tensor(
                      comb.ap().rearrange("q (a b) -> q a b", a=8),
                      sx.ap()[:, 0, h, :].unsqueeze(2).broadcast_to([128, 8, 8]),
                      sx.ap()[:, 1, h, :].unsqueeze(1).broadcast_to([128, 8, 8]), ALU.add),
                  reads=[b_sx], writes=[b_comb])
            P.ins("v", lambda eng, h=h: eng.max(sc8[:, h, :], comb[:, :]),
                  reads=[b_comb], writes=[b_sc8])
            P.ins("v", lambda eng, h=h: eng.max_index(pku[:, :], sc8[:, h, :], comb[:, :]),
                  reads=[b_comb, b_sc8], writes=[b_pku])
            def mkab(eng, h=h):
                eng.tensor_scalar(ixu[:, :], pku[:, :], 3, None, ALU.logical_shift_right)
                return eng.tensor_copy(pka[:, 0, :], ixu[:, :])
            P.ins("v", mkab, reads=[b_pku], writes=[b_pka, b_ixu])
            def mkb2(eng, h=h):
                eng.tensor_scalar(ixu[:, :], pku[:, :], 7, None, ALU.bitwise_and)
                return eng.tensor_copy(pka[:, 1, :], ixu[:, :])
            P.ins("v", mkb2, reads=[b_pku], writes=[b_pka, b_ixu])
            for p in range(2):
                P.ins("v", lambda eng, p=p: eng.tensor_tensor(
                          oh.ap().rearrange("q (a b) -> q a b", a=8),
                          pka.ap()[:, p, :].unsqueeze(2).broadcast_to([128, 8, 8]),
                          iota8.ap().unsqueeze(1).broadcast_to([128, 8, 8]), ALU.is_equal),
                      reads=[b_pka, b_iota8], writes=[b_oh])
                P.ins("v", lambda eng, p=p, h=h: eng.tensor_tensor(
                          oh.ap().rearrange("q (a b) -> q a b", a=8), 
                          oh.ap().rearrange("q (a b) -> q a b", a=8),
                          ixf.ap()[:, p, h, :].unsqueeze(1).broadcast_to([128, 8, 8]), ALU.mult),
                      reads=[b_oh, b_ixf], writes=[b_oh])
                P.ins("v", lambda eng, p=p: eng.tensor_reduce(
                          pka[:, p, :], oh.ap().rearrange("q (a b) -> q a b", a=8), AX.X, ALU.add),
                      reads=[b_oh], writes=[b_pka])
            P.ins("v", lambda eng, h=h: eng.tensor_scalar(
                      pka[:, 0, :], pka[:, 0, :], 128.0, None, ALU.mult),
                  reads=[b_pka], writes=[b_pka])
            P.ins("v", lambda eng, h=h: eng.tensor_tensor(
                      pka[:, 0, :], pka[:, 0, :], pka[:, 1, :], ALU.add),
                  reads=[b_pka], writes=[b_pka])
            P.ins("v", lambda eng, h=h: eng.tensor_copy(idxs[:, h, :], pka[:, 0, :]),
                  reads=[b_pka], writes=[b_idxs])
        P.ins("v", lambda eng: eng.tensor_reduce(scs[:, 0:4], sc8.ap(), AX.X, ALU.max),
              reads=[b_sc8], writes=[b_scs])
        P.ins("v", lambda eng: eng.tensor_tensor(
                  sc8.ap(), sc8.ap(),
                  scs.ap()[:, 0:4].unsqueeze(2).broadcast_to([128, 4, 8]), ALU.subtract),
              reads=[b_scs], writes=[b_sc8])
        P.ins("s", lambda eng: eng.activation(sc8.ap(), sc8.ap(), AF.Exp),
              reads=[b_sc8], writes=[b_sc8])
        P.ins("v", lambda eng: eng.tensor_reduce(scs[:, 4:8], sc8.ap(), AX.X, ALU.add),
              reads=[b_sc8], writes=[b_scs])
        P.ins("v", lambda eng: eng.reciprocal(scs[:, 4:8], scs[:, 4:8]),
              reads=[b_scs], writes=[b_scs])
        P.ins("v", lambda eng: eng.tensor_tensor(
                  sc8.ap(), sc8.ap(),
                  scs.ap()[:, 4:8].unsqueeze(2).broadcast_to([128, 4, 8]), ALU.mult),
              reads=[b_scs], writes=[b_sc8])
        def mkty(eng, t0=t0):
            last = None
            for c in range(NCH):
                last = eng.transpose(ps_trb[:, c * 128:(c + 1) * 128],
                                     yb[:, c, t0:t0 + 128], C.ident[:])
            return last
        P.ins("t", mkty, reads=b_yb + [C.b_ident], writes=[b_ps_trb])
        P.ins("v", lambda eng: eng.tensor_copy(ytm[:, :], ps_trb[:, :]),
              reads=[b_ps_trb], writes=[b_ytm])
        P.fence("v", ps_trb[0:1, 0:1], b_ps_trb, C.fscr[0:1, 0:1])
        P.ins("v", lambda eng: eng.memset(acct[:, :], 0.0), writes=[b_acct])
        for sl_i in range(32):
            h, k = sl_i // 8, sl_i % 8
            gt, bgt = gth[sl_i % 2]
            P.ins("g", lambda eng, h=h, k=k, gt=gt: eng.indirect_dma_start(
                      out=gt[:, :], out_offset=None, in_=hb["f1_emb"],
                      in_offset=IndirectOffsetOnAxis(idxs[:, h, k:k + 1], 0)),
                  reads=[b_idxs], writes=[bgt],
                  sem=(C.dN if sl_i % 2 == 0 else C.dO), inc=16)
            P.ins("v", lambda eng, gt=gt: eng.tensor_tensor(junk[:, :], gt[:, 0:1024],
                                                            ytm[:, :], ALU.mult),
                  reads=[bgt, b_ytm], writes=[b_junk])
            P.ins("s", lambda eng: eng.activation(junk[:, :], junk[:, :], AF.Copy,
                                                  accum_out=scs[:, 0:1]),
                  reads=[b_junk], writes=[b_junk, b_scs])
            P.ins("s", lambda eng: eng.activation(scs[:, 1:2], scs[:, 0:1], AF.Copy),
                  reads=[b_scs], writes=[b_scs])
            P.ins("s", lambda eng: eng.activation(scs[:, 2:3], scs[:, 0:1], AF.Silu),
                  reads=[b_scs], writes=[b_scs])
            P.ins("v", lambda eng, h=h, k=k: eng.tensor_tensor(
                      wcol[:, h, k:k + 1], scs[:, 2:3], sc8[:, h, k:k + 1], ALU.mult),
                  reads=[b_scs, b_sc8], writes=[b_wcol])
            P.ins("v", lambda eng, gt=gt, h=h, k=k: eng.scalar_tensor_tensor(
                      acct[:, :], gt[:, 1024:2048], wcol[:, h, k:k + 1], acct[:, :],
                      ALU.mult, ALU.add),
                  reads=[bgt, b_wcol], writes=[b_acct])
        if parts == "mlp":
            for c in range(NCH):
                P.ins("v", lambda eng, c=c, t0=t0: eng.memset(yf[:, c, t0:t0 + 128], 0.0),
                      writes=[b_yf[c]])
            continue
        for c in range(NCH):
            P.ins("t", lambda eng, c=c: eng.transpose(
                      ps_sim[:, 0:128], acct[:, c * 128:(c + 1) * 128], C.identf[:]),
                  reads=[b_acct, C.b_ident], writes=[b_ps_sim])
            P.ins("v", lambda eng, c=c, t0=t0: eng.tensor_scalar(
                      yf[:, c, t0:t0 + 128], ps_sim[:, 0:128], 0.5, None, ALU.mult),
                  reads=[b_ps_sim], writes=[b_yf[c]])
            P.fence("v", ps_sim[0:1, 0:1], b_ps_sim, C.fscr[0:1, 0:1])
    P.barrier()

    for fg in (range(4) if parts != "exp" else []):
        for j in range(8):
            jj = fg * 8 + j
            P.ins("g", lambda eng, jj=jj, j=j: eng.dma_start(
                      out=wgq[j][0][:, :, :], in_=hb["f1_wg"][jj].rearrange("k p f -> p k f")),
                  writes=[wgq[j][1]], sem=C.dC, inc=16)
            P.ins("g", lambda eng, jj=jj, j=j: eng.dma_start(
                      out=wuq[j][0][:, :, :], in_=hb["f1_wu"][jj].rearrange("k p f -> p k f")),
                  writes=[wuq[j][1]], sem=C.dC, inc=16)
        for tt in range(8):
            sl = slice(tt * 256, tt * 256 + 256)
            for j in range(8):
                def mkg(eng, j=j, sl=sl):
                    last = None
                    for k in range(NCH):
                        last = eng.matmul(ps_gu[:, 0:256], lhsT=wgq[j][0][:, k, :],
                                          rhs=yb[:, k, sl], start=(k == 0), stop=(k == NCH - 1))
                    return last
                P.ins("t", mkg, reads=[wgq[j][1]] + b_yb, writes=[b_ps_gu])
                P.ins("s", lambda eng, j=j: eng.activation(junk[:, 0:256],
                                                           ps_gu[:, 0:256], AF.Silu),
                      reads=[b_ps_gu], writes=[b_junk])
                P.fence("s", ps_gu[0:1, 0:1], b_ps_gu, C.fscr[0:1, 1:2])
                def mku(eng, j=j, sl=sl):
                    last = None
                    for k in range(NCH):
                        last = eng.matmul(ps_gu[:, 256:512], lhsT=wuq[j][0][:, k, :],
                                          rhs=yb[:, k, sl], start=(k == 0), stop=(k == NCH - 1))
                    return last
                P.ins("t", mku, reads=[wuq[j][1]] + b_yb, writes=[b_ps_gu])
                P.ins("v", lambda eng, j=j: eng.tensor_tensor(
                          hbuf[:, j * 256:j * 256 + 256], junk[:, 0:256],
                          ps_gu[:, 256:512], ALU.mult),
                      reads=[b_ps_gu, b_junk], writes=[b_hb])
                P.fence("v", ps_gu[0:1, 256:257], b_ps_gu, C.fscr[0:1, 0:1])
            for f in range(8):
                for j in range(8):
                    jj2 = fg * 8 + j
                    s_ = (f * 8 + j) % 2
                    wdt, wdb_ = wdj[s_]
                    P.ins("g", lambda eng, f=f, jj2=jj2, wdt=wdt: eng.dma_start(
                              out=wdt[:, :], in_=hb["f1_wd"][f, jj2]),
                          writes=[wdb_], sem=(C.dA if s_ == 0 else C.dB), inc=16)
                    P.ins("t", lambda eng, f=f, j=j, wdt=wdt: eng.matmul(
                              ps_mlp[:, f, :], lhsT=wdt[:, :],
                              rhs=hbuf[:, j * 256:j * 256 + 256],
                              start=(j == 0), stop=(j == 7)),
                          reads=[wdb_, b_hb], writes=[b_ps_mlp])
            for f in range(8):
                P.ins("v", lambda eng, f=f, sl=sl: eng.scalar_tensor_tensor(
                          yf[:, f, sl], ps_mlp[:, f, :], 0.5, yf[:, f, sl],
                          ALU.mult, ALU.add),
                      reads=[b_ps_mlp], writes=[b_yf[f]])
            P.fence("v", ps_mlp[0:1, 0, 0:1], b_ps_mlp, C.fscr[0:1, 0:1])
    P.barrier()
    for tt in range(8):
        sl = slice(256 * tt, 256 * tt + 256)
        P.ins("y", lambda eng, sl=sl: eng.dma_start(out=xstr[:, :, :], in_=xacc_in_ap[:, :, sl]),
              reads=[b_xacc], writes=[b_xstr], sem=C.dJ, inc=16)
        for c in range(NCH):
            P.ins("v", lambda eng, c=c, sl=sl: eng.tensor_tensor(
                      xstr[:, c, :], xstr[:, c, :], yf[:, c, sl], ALU.add),
                  reads=[b_yf[c]], writes=[b_xstr])
        P.ins("y", lambda eng, sl=sl: eng.dma_start(out=xacc_out_ap[:, :, sl], in_=xstr[:, :, :]),
              reads=[b_xstr], writes=[b_xacc], sem=C.dJ, inc=16)
    P.barrier()


def build_f1_launch(shared_host, percore_sample, parts="both", dbg_mlp=False):
    import contextlib
    nc = bass.Bass("TRN2", target_bir_lowering=False,
                   detect_race_conditions=RACE_DETECT)
    P = Prog(nc)
    xin = nc.dram_tensor("xin", [128, NCH, T], F32, kind="ExternalInput").ap()
    xout = nc.dram_tensor("xout", [128, NCH, T], F32, kind="ExternalOutput").ap()
    hb = {}
    for srcd in (shared_host, percore_sample):
        for k, v in srcd.items():
            hb[k] = nc.dram_tensor(k, list(v.shape), _np2dt(v), kind="ExternalInput").ap()
    st = contextlib.ExitStack()
    with st:
        C = alloc_common(nc, st, P)
        lnws = alloc_lnws(C, st)
        load_common(C, P, hb)
        b_xin = Buf("xin")
        b_xacc = Buf("xacc")
        emit_f1(C, P, st, hb, xin, b_xin, xin, xout, b_xacc, lnws, parts=parts,
                dbg_mlp=dbg_mlp)
        P.wait("y", C.dY, C.dY.v)
        with nc.Block() as block:
            P.replay(block)
    return nc



A1_NH, A1_HD, A1_C, A1_NCHK = 8, 128, 128, 16
A1_GAMMAS = (1.0 - np.exp(np.linspace(np.log(1 / 32), np.log(1 / 512), 8))).astype(np.float64)


def a1_host_shared(inp):
    d = {}
    wq = np.asarray(inp["L1_attn_Wq"], np.float32)
    wk = np.asarray(inp["L1_attn_Wk"], np.float32)
    wv = np.asarray(inp["L1_attn_Wv"], np.float32)
    wg = np.asarray(inp["L1_attn_W_G"], np.float32)
    wo = np.asarray(inp["L1_attn_W_O"], np.float32)
    d["a1_wq"] = np.stack([np.stack([_tiles_fm(wq[l, h])[0] for h in range(8)]) for l in range(2)])
    d["a1_wk"] = np.stack([np.stack([_tiles_fm(wk[l, h])[0] for h in range(8)]) for l in range(2)])
    d["a1_wv"] = np.stack([np.stack([_tiles_fm(wv[l, h])[0] for h in range(8)]) for l in range(2)])
    d["a1_wg"] = np.stack([_tiles_fm(wg[l]) for l in range(2)])
    d["a1_wo"] = np.stack([_tiles_fm(wo[l]) for l in range(2)])
    w1 = np.asarray(inp["L1_attn_ffn_w1"], np.float32)
    w2 = np.asarray(inp["L1_attn_ffn_w2"], np.float32)
    d["a1_w1"] = np.stack([_tiles_fm(w1[l]) for l in range(2)])
    d["a1_w2"] = np.stack([_tiles_fm(w2[l]) for l in range(2)])
    d["a1_b1"] = np.asarray(inp["L1_attn_ffn_b1"], np.float32).reshape(2, 1, 4096).astype(BF)
    d["a1_b2"] = np.stack([_fmcol(np.asarray(inp["L1_attn_ffn_b2"], np.float32)[l]) for l in range(2)])
    for nm in ("ln1_w", "ln1_b", "ln2_w", "ln2_b"):
        d[f"a1_{nm}"] = np.stack([_fmcol(np.asarray(inp[f"L1_attn_{nm}"], np.float32)[l]) for l in range(2)])
    gnw = np.asarray(inp["L1_attn_gn_w"], np.float32)
    gnb = np.asarray(inp["L1_attn_gn_b"], np.float32)
    d["a1_gnw"] = np.ascontiguousarray(gnw.reshape(2, 8, 128).transpose(0, 2, 1))
    d["a1_gnb"] = np.ascontiguousarray(gnb.reshape(2, 8, 128).transpose(0, 2, 1))
    g = A1_GAMMAS
    kj = np.arange(128)[:, None]
    qi = np.arange(128)[None, :]
    DT = np.zeros((128, 8, 128), np.float64)
    for h in range(8):
        DT[:, h, :] = np.where(qi >= kj, g[h] ** (qi - kj), 0.0)
    d["a1_dt"] = DT.astype(np.float32).astype(BF)
    d["a1_gcol"] = np.stack([g[h] ** (127 - np.arange(128)) for h in range(8)], 1).astype(np.float32)
    grow = np.zeros((128, 8, 128), np.float64)
    for h in range(8):
        grow[:, h, :] = (g[h] ** (np.arange(128) + 1))[None, :]
    d["a1_grow"] = grow.astype(np.float32).astype(BF)
    R2 = np.zeros((128, 128), np.float32)
    for j in range(64):
        R2[2 * j, 2 * j + 1] = -1.0
        R2[2 * j + 1, 2 * j] = 1.0
    d["a1_rot2"] = np.ascontiguousarray(R2.T).astype(BF)
    return d


def _a1_xpos_tables(s):
    HD2 = 64
    pos = np.arange(s, s + T, dtype=np.float64)
    base = (np.arange(0, 128, 2, dtype=np.float64) + 0.4 * 128) / (1.4 * 128)
    sc = base[:, None] ** (pos[None, :] / 512.0)
    inv_freq = 1.0 / 10000.0 ** (np.arange(HD2, dtype=np.float64) / HD2)
    ang = pos[None, :] * inv_freq[:, None]
    cos, sin = np.cos(ang), np.sin(ang)
    def dup(a):
        out = np.zeros((128, a.shape[1]), np.float64)
        out[0::2] = a
        out[1::2] = a
        return out
    cs = dup(cos * sc); ss = dup(sin * sc)
    ci = dup(cos / sc); si = dup(sin / sc)
    return cs, ss, ci, si


def a1_host_percore(core):
    d = {}
    s = core * T
    cs, ss, ci, si = _a1_xpos_tables(s)
    d["a1_tq"] = np.stack([cs, ss], 1).astype(np.float32).astype(BF)
    d["a1_tk"] = np.stack([ci, si], 1).astype(np.float32).astype(BF)
    wst = np.zeros((8, 8), np.float64)
    for j in range(core):
        for h in range(8):
            wst[j, h] = A1_GAMMAS[h] ** (2048.0 * (core - 1 - j))
    d["a1_wst"] = np.tile(wst.astype(np.float32)[None, :, :], (128, 1, 1))
    return d


def emit_a1(C, P, st, hb, xin_ap, b_xin, xout_ap, b_xout, dbg=False, nlayers=2):
    nc = C.nc
    NH, CH1, NK = A1_NH, A1_C, A1_NCHK

    kind = "ExternalOutput" if dbg else "Internal"
    xa = nc.dram_tensor("a1_xa", [128, NCH, T], F32, kind=kind).ap()
    ya = nc.dram_tensor("a1_ya", [128, NCH, T], F32, kind=kind).ap()
    gyd = nc.dram_tensor("a1_gyd", [128, NCH, T], BF16, kind=kind).ap()
    b_xa, b_ya, b_gyd = Buf("a1_xa"), Buf("a1_ya"), Buf("a1_gyd")

    lnws = C.a1_lnws

    xn, _ = _sb(C, st, "a1_xn", [128, NCH, T], BF16)
    b_xn = [Buf(f"a1_xn{c}") for c in range(NCH)]
    big, _ = _sb(C, st, "a1_big", [128, NCH, 2 * T], BF16)
    krot = big.ap()[:, :, 0:T]
    vtt = big.ap()[:, :, T:2 * T]
    accf = big.ap().bitcast(F32)
    b_big = Buf("a1_big")
    tqb, b_tq = _sb(C, st, "a1_tqb", [128, 2, T], BF16)
    tkb, b_tk = _sb(C, st, "a1_tkb", [128, 2, T], BF16)
    wpa, b_wpa = _sb(C, st, "a1_wpa", [128, 8, 8, 128], BF16)
    wpb, b_wpb = _sb(C, st, "a1_wpb", [128, 8, 8, 128], BF16)
    wos = [_sb(C, st, f"a1_wos{i}", [128, 8, 128], BF16) for i in range(2)]
    sfp, b_sfp = _sb(C, st, "a1_sfp", [128, NH * 128], F32)
    sbf, b_sbf = _sb(C, st, "a1_sbf", [128, NH * 128], BF16)

    dtt, b_dtt = _sb(C, st, "a1_dt", [128, 8, 128], BF16)
    gcol, b_gcol = _sb(C, st, "a1_gcol", [128, 8], F32)
    growb, b_grow = _sb(C, st, "a1_grow", [128, 8, 128], BF16)
    rot2, b_rot2 = _sb(C, st, "a1_rot2", [128, 128], BF16)
    wstb, b_wst = _sb(C, st, "a1_wst", [128, 8, 8], F32)
    gnwb, b_gnw = _sb(C, st, "a1_gnw", [128, 2, 2, 8], F32)
    lnwb, b_lnw = _sb(C, st, "a1_lnw", [128, 2, 4, 8], F32)
    b2c, b_b2c = _sb(C, st, "a1_b2c", [128, 2, 8], F32)
    b1r, b_b1r = _sb(C, st, "a1_b1r", [1, 4096], BF16)
    ksb, b_ksb = _sb(C, st, "a1_ksb", [128, 2, 128], BF16)
    vsb, b_vsb = _sb(C, st, "a1_vsb", [128, 2, 128], BF16)
    qrsb, b_qrsb = _sb(C, st, "a1_qrsb", [128, 2, 128], BF16)
    qhat, b_qhat = _sb(C, st, "a1_qhat", [128, 2, 128], BF16)
    sdts, b_sdts = _sb(C, st, "a1_sdts", [128, 2, 128], BF16)
    khts, b_khts = _sb(C, st, "a1_khts", [128, 2, 128], BF16)
    tm1, b_tm1 = _sb(C, st, "a1_tm1", [128, 2, 128], F32)
    ysb, b_ysb = _sb(C, st, "a1_ysb", [128, 128], BF16)
    ysq, b_ysq = _sb(C, st, "a1_ysq", [128, 128], BF16)
    gsil, b_gsil = _sb(C, st, "a1_gsil", [128, 8, 128], BF16)
    scr4, _ = _sb(C, st, "a1_scr4", [128, 2048], BF16)
    gysb = [(scr4.ap()[:, i * 1024:(i + 1) * 1024].rearrange("p (c t) -> p c t", c=8),
             Buf(f"a1_gy{i}")) for i in range(2)]
    gnrow, b_gnrow = _sb(C, st, "a1_gnrow", [1, 512], F32)
    xstr, b_xstr = _sb(C, st, "a1_xstr", [128, NCH, 256], F32)
    gstr0 = _sb(C, st, "a1_gst0", [128, NCH, 256], BF16)
    gstr = [gstr0, gstr0]
    hbuf = scr4.ap().rearrange("p (j t) -> p j t", j=8)
    b_hbuf = Buf("a1_hbuf")

    psP, b_psP = _ps(C, st, "a1_psP", [128, 512])
    psR, b_psR = _ps(C, st, "a1_psR", [128, 512])
    psY, b_psY = _ps(C, st, "a1_psY", [128, 512])
    psS2, b_psS2 = _ps(C, st, "a1_psS2", [128, 512])
    psT, b_psT = _ps(C, st, "a1_psT", [128, 1024], BF16)
    pbc = [(psP.ap()[:, 0:512], b_psP), (psR.ap()[:, 0:512], b_psR)]

    GCONST = [float(g) for g in A1_GAMMAS]
    G128 = [float(g ** 128) for g in A1_GAMMAS]

    P.dmas("y", [
        lambda eng: eng.dma_start(out=tqb[:], in_=hb["a1_tq"]),
        lambda eng: eng.dma_start(out=tkb[:], in_=hb["a1_tk"]),
        lambda eng: eng.dma_start(out=dtt[:], in_=hb["a1_dt"]),
        lambda eng: eng.dma_start(out=gcol[:], in_=hb["a1_gcol"]),
        lambda eng: eng.dma_start(out=growb[:], in_=hb["a1_grow"]),
        lambda eng: eng.dma_start(out=rot2[:], in_=hb["a1_rot2"]),
        lambda eng: eng.dma_start(out=wstb[:], in_=hb["a1_wst"]),
        lambda eng: eng.dma_start(out=gnwb[:, :, 0, :], in_=hb["a1_gnw"].rearrange("l p h -> p l h")),
        lambda eng: eng.dma_start(out=gnwb[:, :, 1, :], in_=hb["a1_gnb"].rearrange("l p h -> p l h")),
        lambda eng: eng.dma_start(out=lnwb[:, :, 0, :], in_=hb["a1_ln1_w"].rearrange("l p h -> p l h")),
        lambda eng: eng.dma_start(out=lnwb[:, :, 1, :], in_=hb["a1_ln1_b"].rearrange("l p h -> p l h")),
        lambda eng: eng.dma_start(out=lnwb[:, :, 2, :], in_=hb["a1_ln2_w"].rearrange("l p h -> p l h")),
        lambda eng: eng.dma_start(out=lnwb[:, :, 3, :], in_=hb["a1_ln2_b"].rearrange("l p h -> p l h")),
        lambda eng: eng.dma_start(out=b2c[:], in_=hb["a1_b2"].rearrange("l p h -> p l h")),
    ], writes=[b_tq, b_tk, b_dtt, b_gcol, b_grow, b_rot2, b_wst, b_gnw, b_lnw,
               b_b2c], sem=C.dL)
    P.barrier()

    def p0_pre(tt, sl, w):
        P.ins("y", lambda eng, sl=sl: eng.dma_start(out=xstr[:, :, :],
                                                    in_=xin_ap[:, :, sl]),
              reads=[b_xin], writes=[b_xstr], sem=C.dJ, inc=16)
        return (lambda c: xstr[:, c, :]), [b_xstr] * NCH
    def p0_tile(tt, sl, w):
        P.ins("y", lambda eng, sl=sl: eng.dma_start(out=xa[:, :, sl],
                                                    in_=xstr[:, :, :]),
              reads=[b_xstr], writes=[b_xa], sem=C.dJ, inc=16)
    emit_ln(C, P, lnws, None, None, lambda c: xstr[:, c, :], [b_xstr] * NCH,
            pbc, n=T, tstep=256, pre_tile=p0_pre, tile_cb=p0_tile, out_local=True)
    P.barrier()

    for li in range(nlayers):
        def p1_pre(tt, sl, w):
            P.ins("y", lambda eng, sl=sl: eng.dma_start(out=xstr[:, :, :],
                                                        in_=xa[:, :, sl]),
                  reads=[b_xa], writes=[b_xstr], sem=C.dJ, inc=16)
            return (lambda c: xstr[:, c, :]), [b_xstr] * NCH
        emit_ln(C, P, lnws, None, None,
                lambda c, li=li: xn[:, c, :], b_xn, pbc, n=T, tstep=256,
                pre_tile=p1_pre,
                gamma=lambda c, li=li: lnwb[:, li, 0, c:c + 1],
                beta=lambda c, li=li: lnwb[:, li, 1, c:c + 1], b_gb=b_lnw,
                out_local=False)
        P.barrier()

        P.dmas("g", [
            (lambda eng, h=h, li=li: eng.dma_start(
                out=wpa[:, h, :, :], in_=hb["a1_wk"][li, h].rearrange("k p f -> p k f")))
            for h in range(8)
        ] + [
            (lambda eng, h=h, li=li: eng.dma_start(
                out=wpb[:, h, :, :], in_=hb["a1_wv"][li, h].rearrange("k p f -> p k f")))
            for h in range(8)
        ], writes=[b_wpa, b_wpb], sem=C.dC)
        P.ins("v", lambda eng: eng.memset(sfp[:, :], 0.0), writes=[b_sfp])

        for t in range(NK):
            tc = slice(t * CH1, t * CH1 + CH1)
            for h in range(NH):
                sl2 = h % 2
                def mkk(eng, h=h, tc=tc):
                    last = None
                    for c in range(NCH):
                        last = eng.matmul(psP[:, (h % 4) * 128:(h % 4) * 128 + 128],
                                          lhsT=wpa[:, h, c, :], rhs=xn[:, c, tc],
                                          start=(c == 0), stop=(c == NCH - 1))
                    return last
                P.ins("t", mkk, reads=[b_wpa] + b_xn, writes=[b_psP])
                P.ins("s", lambda eng, h=h, sl2=sl2: eng.activation(
                          ksb[:, sl2, :], psP[:, (h % 4) * 128:(h % 4) * 128 + 128], AF.Copy),
                      reads=[b_psP], writes=[b_ksb])
                P.fence("s", psP[0:1, 0:1], b_psP, C.fscr[0:1, 1:2])
                P.ins("t", lambda eng, sl2=sl2: eng.matmul(
                          psR[:, sl2 * 128:sl2 * 128 + 128], lhsT=rot2[:, :],
                          rhs=ksb[:, sl2, :], start=True, stop=True),
                      reads=[b_rot2, b_ksb], writes=[b_psR])
                P.ins("v", lambda eng, sl2=sl2, tc=tc: eng.tensor_tensor(
                          tm1[:, sl2, :], ksb[:, sl2, :], tkb[:, 0, tc], ALU.mult),
                      reads=[b_ksb, b_tk], writes=[b_tm1])
                P.ins("v", lambda eng, sl2=sl2, tc=tc: eng.tensor_tensor(
                          ysq[:, :], psR[:, sl2 * 128:sl2 * 128 + 128], tkb[:, 1, tc], ALU.mult),
                      reads=[b_psR, b_tk], writes=[b_ysq])
                P.fence("v", psR[0:1, 0:1], b_psR, C.fscr[0:1, 0:1])
                P.ins("v", lambda eng, h=h, tc=tc, sl2=sl2: eng.tensor_tensor(
                          krot[:, h, tc], tm1[:, sl2, :], ysq[:, :], ALU.add),
                      reads=[b_tm1, b_ysq], writes=[b_big])
                def mkv(eng, h=h, tc=tc):
                    last = None
                    for c in range(NCH):
                        last = eng.matmul(psP[:, (h % 4) * 128:(h % 4) * 128 + 128],
                                          lhsT=wpb[:, h, c, :], rhs=xn[:, c, tc],
                                          start=(c == 0), stop=(c == NCH - 1))
                    return last
                P.ins("t", mkv, reads=[b_wpb] + b_xn, writes=[b_psP])
                P.ins("s", lambda eng, h=h, sl2=sl2: eng.activation(
                          vsb[:, sl2, :], psP[:, (h % 4) * 128:(h % 4) * 128 + 128], AF.Copy),
                      reads=[b_psP], writes=[b_vsb])
                P.fence("s", psP[0:1, 0:1], b_psP, C.fscr[0:1, 1:2])
                P.ins("t", lambda eng, sl2=sl2: eng.transpose(
                          psT[:, sl2 * 128:sl2 * 128 + 128], vsb[:, sl2, :], C.ident[:]),
                      reads=[b_vsb, C.b_ident], writes=[b_psT])
                P.ins("v", lambda eng, h=h, tc=tc, sl2=sl2: eng.tensor_copy(
                          vtt[:, h, tc], psT[:, sl2 * 128:sl2 * 128 + 128]),
                      reads=[b_psT], writes=[b_big])
                P.fence("v", psT[0:1, 0:1], b_psT, C.fscr[0:1, 0:1])
                P.ins("t", lambda eng, h=h, tc=tc, sl2=sl2: eng.transpose(
                          psT[:, 256 + sl2 * 128:256 + sl2 * 128 + 128], krot[:, h, tc],
                          C.ident[:]),
                      reads=[b_big, C.b_ident], writes=[b_psT])
                P.ins("v", lambda eng, h=h, sl2=sl2: eng.tensor_scalar(
                          khts[:, sl2, :], psT[:, 256 + sl2 * 128:256 + sl2 * 128 + 128],
                          gcol[:, h:h + 1], None, ALU.mult),
                      reads=[b_psT, b_gcol], writes=[b_khts])
                P.fence("v", psT[0:1, 256:257], b_psT, C.fscr[0:1, 0:1])
                P.ins("t", lambda eng, h=h, tc=tc, sl2=sl2: eng.matmul(
                          psS2[:, sl2 * 128:sl2 * 128 + 128], lhsT=khts[:, sl2, :],
                          rhs=vtt[:, h, tc], start=True, stop=True),
                      reads=[b_khts, b_big], writes=[b_psS2])
                P.ins("v", lambda eng, h=h, sl2=sl2: eng.scalar_tensor_tensor(
                          sfp[:, h * 128:h * 128 + 128], sfp[:, h * 128:h * 128 + 128],
                          G128[h], psS2[:, sl2 * 128:sl2 * 128 + 128], ALU.mult, ALU.add),
                      reads=[b_psS2, b_sfp], writes=[b_sfp])
                P.fence("v", psS2[0:1, 0:1], b_psS2, C.fscr[0:1, 0:1])
        P.barrier()

        ago, b_ago = emit_allgather(C, P, f"a1s{li}_{P.nins}", sfp[:, :], b_sfp,
                                    [128, NH * 128], dt=F32)
        ldS = xstr.ap().rearrange("p c t -> p (c t)")[:, 0:NH * 128]
        b_ldS = b_xstr
        P.ins("v", lambda eng: eng.memset(sfp[:, :], 0.0), writes=[b_sfp])
        for j in range(NCORES):
            P.ins("y", lambda eng, j=j: eng.dma_start(out=ldS[:, :],
                                                      in_=ago[j * 128:(j + 1) * 128, :]),
                  reads=[b_ago], writes=[b_ldS], sem=C.dF, inc=16)
            for h in range(NH):
                P.ins("v", lambda eng, j=j, h=h: eng.scalar_tensor_tensor(
                          sfp[:, h * 128:h * 128 + 128], ldS[:, h * 128:h * 128 + 128],
                          wstb[:, j, h:h + 1], sfp[:, h * 128:h * 128 + 128],
                          ALU.mult, ALU.add),
                      reads=[b_ldS, b_wst], writes=[b_sfp])
        P.ins("s", lambda eng: eng.activation(sbf[:, :], sfp[:, :], AF.Copy),
              reads=[b_sfp], writes=[b_sbf])
        P.dmas("g", [
            (lambda eng, h=h, li=li: eng.dma_start(
                out=wpa[:, h, :, :], in_=hb["a1_wq"][li, h].rearrange("k p f -> p k f")))
            for h in range(8)
        ] + [
            (lambda eng, f=f, li=li: eng.dma_start(
                out=wpb[:, f, :, :], in_=hb["a1_wg"][li, f].rearrange("k p f -> p k f")))
            for f in range(8)
        ], writes=[b_wpa, b_wpb], sem=C.dC)
        P.barrier()

        for t in range(NK):
            tc = slice(t * CH1, t * CH1 + CH1)
            ts2 = t % 2
            for f in range(NCH):
                def mkg(eng, f=f, tc=tc):
                    last = None
                    for c in range(NCH):
                        last = eng.matmul(psP[:, (f % 4) * 128:(f % 4) * 128 + 128],
                                          lhsT=wpb[:, f, c, :], rhs=xn[:, c, tc],
                                          start=(c == 0), stop=(c == NCH - 1))
                    return last
                P.ins("t", mkg, reads=[b_wpb] + b_xn, writes=[b_psP])
                P.ins("s", lambda eng, f=f: eng.activation(
                          gsil[:, f, :], psP[:, (f % 4) * 128:(f % 4) * 128 + 128], AF.Silu),
                      reads=[b_psP], writes=[b_gsil])
                P.fence("s", psP[0:1, 0:1], b_psP, C.fscr[0:1, 1:2])
            for h in range(NH):
                sl2 = h % 2
                def mkq(eng, h=h, tc=tc):
                    last = None
                    for c in range(NCH):
                        last = eng.matmul(psP[:, (h % 4) * 128:(h % 4) * 128 + 128],
                                          lhsT=wpa[:, h, c, :], rhs=xn[:, c, tc],
                                          start=(c == 0), stop=(c == NCH - 1))
                    return last
                P.ins("t", mkq, reads=[b_wpa] + b_xn, writes=[b_psP])
                P.ins("s", lambda eng, h=h, sl2=sl2: eng.activation(
                          ksb[:, sl2, :], psP[:, (h % 4) * 128:(h % 4) * 128 + 128], AF.Copy),
                      reads=[b_psP], writes=[b_ksb])
                P.fence("s", psP[0:1, 0:1], b_psP, C.fscr[0:1, 1:2])
                P.ins("t", lambda eng, sl2=sl2: eng.matmul(
                          psR[:, sl2 * 128:sl2 * 128 + 128], lhsT=rot2[:, :],
                          rhs=ksb[:, sl2, :], start=True, stop=True),
                      reads=[b_rot2, b_ksb], writes=[b_psR])
                P.ins("v", lambda eng, sl2=sl2, tc=tc: eng.tensor_tensor(
                          tm1[:, sl2, :], ksb[:, sl2, :], tqb[:, 0, tc], ALU.mult),
                      reads=[b_ksb, b_tq], writes=[b_tm1])
                P.ins("v", lambda eng, sl2=sl2, tc=tc: eng.tensor_tensor(
                          ysq[:, :], psR[:, sl2 * 128:sl2 * 128 + 128], tqb[:, 1, tc], ALU.mult),
                      reads=[b_psR, b_tq], writes=[b_ysq])
                P.fence("v", psR[0:1, 0:1], b_psR, C.fscr[0:1, 0:1])
                P.ins("v", lambda eng, sl2=sl2: eng.tensor_tensor(
                          qrsb[:, sl2, :], tm1[:, sl2, :], ysq[:, :], ALU.add),
                      reads=[b_tm1, b_ysq], writes=[b_qrsb])
                P.ins("v", lambda eng, h=h, sl2=sl2: eng.tensor_tensor(
                          qhat[:, sl2, :], qrsb[:, sl2, :], growb[:, h, :], ALU.mult),
                      reads=[b_qrsb, b_grow], writes=[b_qhat])
                P.ins("t", lambda eng, h=h, tc=tc, sl2=sl2: eng.matmul(
                          psR[:, 256 + sl2 * 128:256 + sl2 * 128 + 128],
                          lhsT=krot[:, h, tc], rhs=qrsb[:, sl2, :], start=True, stop=True),
                      reads=[b_big, b_qrsb], writes=[b_psR])
                P.ins("v", lambda eng, h=h, sl2=sl2: eng.tensor_tensor(
                          sdts[:, sl2, :], psR[:, 256 + sl2 * 128:256 + sl2 * 128 + 128],
                          dtt[:, h, :], ALU.mult),
                      reads=[b_psR, b_dtt], writes=[b_sdts])
                P.fence("v", psR[0:1, 256:257], b_psR, C.fscr[0:1, 0:1])
                def mky(eng, h=h, tc=tc, sl2=sl2):
                    eng.matmul(psY[:, sl2 * 128:sl2 * 128 + 128],
                               lhsT=sbf[:, h * 128:h * 128 + 128], rhs=qhat[:, sl2, :],
                               start=True, stop=False)
                    return eng.matmul(psY[:, sl2 * 128:sl2 * 128 + 128],
                                      lhsT=vtt[:, h, tc], rhs=sdts[:, sl2, :],
                                      start=False, stop=True)
                P.ins("t", mky, reads=[b_sbf, b_qhat, b_big, b_sdts], writes=[b_psY])
                P.ins("t", lambda eng, h=h, tc=tc, sl2=sl2: eng.transpose(
                          psT[:, 256 + sl2 * 128:256 + sl2 * 128 + 128], krot[:, h, tc],
                          C.ident[:]),
                      reads=[b_big, C.b_ident], writes=[b_psT])
                P.ins("v", lambda eng, h=h, sl2=sl2: eng.tensor_scalar(
                          khts[:, sl2, :], psT[:, 256 + sl2 * 128:256 + sl2 * 128 + 128],
                          gcol[:, h:h + 1], None, ALU.mult),
                      reads=[b_psT, b_gcol], writes=[b_khts])
                P.fence("v", psT[0:1, 256:257], b_psT, C.fscr[0:1, 0:1])
                P.ins("t", lambda eng, h=h, tc=tc, sl2=sl2: eng.matmul(
                          psS2[:, sl2 * 128:sl2 * 128 + 128], lhsT=khts[:, sl2, :],
                          rhs=vtt[:, h, tc], start=True, stop=True),
                      reads=[b_khts, b_big], writes=[b_psS2])
                P.ins("v", lambda eng, h=h, sl2=sl2: eng.scalar_tensor_tensor(
                          sfp[:, h * 128:h * 128 + 128], sfp[:, h * 128:h * 128 + 128],
                          G128[h], psS2[:, sl2 * 128:sl2 * 128 + 128], ALU.mult, ALU.add),
                      reads=[b_psS2, b_sfp], writes=[b_sfp])
                P.fence("v", psS2[0:1, 0:1], b_psS2, C.fscr[0:1, 0:1])
                P.ins("s", lambda eng, h=h: eng.activation(
                          sbf[:, h * 128:h * 128 + 128], sfp[:, h * 128:h * 128 + 128],
                          AF.Copy),
                      reads=[b_sfp], writes=[b_sbf])
                P.ins("s", lambda eng, sl2=sl2: eng.activation(
                          ysb[:, :], psY[:, sl2 * 128:sl2 * 128 + 128], AF.Copy),
                      reads=[b_psY], writes=[b_ysb])
                P.ins("s", lambda eng: eng.activation(ysq[:, :], ysb[:, :], AF.Square),
                      reads=[b_ysb], writes=[b_ysq])
                P.fence("s", psY[0:1, 0:1], b_psY, C.fscr[0:1, 1:2])
                def mkst(eng):
                    eng.matmul(lnws["ps_st"][0:1, 0:128], lhsT=C.bones[:, 0:1],
                               rhs=ysb[:, :], start=True, stop=True)
                    return eng.matmul(lnws["ps_st"][32:33, 0:128], lhsT=C.bones[:, 1:2],
                                      rhs=ysq[:, :], start=True, stop=True)
                P.ins("t", mkst, reads=[b_ysb, b_ysq, C.b_ones], writes=[lnws["b_ps_st"]])
                P.ins("v", lambda eng: eng.tensor_scalar(
                          gnrow[0:1, 0:128], lnws["ps_st"][0:1, 0:128], 1.0 / 128, None,
                          ALU.mult),
                      reads=[lnws["b_ps_st"]], writes=[b_gnrow])
                P.ins("v", lambda eng: eng.tensor_scalar(
                          gnrow[0:1, 128:256], lnws["ps_st"][32:33, 0:128], 1.0 / 128, None,
                          ALU.mult),
                      reads=[lnws["b_ps_st"]], writes=[b_gnrow])
                P.fence("v", lnws["ps_st"][0:1, 0:1], lnws["b_ps_st"], C.fscr[0:1, 0:1])
                P.ins("s", lambda eng: eng.activation(gnrow[0:1, 256:384],
                                                      gnrow[0:1, 0:128], AF.Square),
                      reads=[b_gnrow], writes=[b_gnrow])
                P.ins("v", lambda eng: eng.tensor_tensor(
                          gnrow[0:1, 128:256], gnrow[0:1, 128:256], gnrow[0:1, 256:384],
                          ALU.subtract),
                      reads=[b_gnrow], writes=[b_gnrow])
                P.ins("s", lambda eng: eng.activation(gnrow[0:1, 128:256],
                                                      gnrow[0:1, 128:256], AF.Ln,
                                                      bias=C.eps),
                      reads=[b_gnrow, C.b_consts], writes=[b_gnrow])
                P.ins("s", lambda eng: eng.activation(gnrow[0:1, 128:256],
                                                      gnrow[0:1, 128:256], AF.Exp,
                                                      scale=-0.5),
                      reads=[b_gnrow], writes=[b_gnrow])
                P.ins("v", lambda eng: eng.tensor_tensor(
                          gnrow[0:1, 0:128], gnrow[0:1, 0:128], gnrow[0:1, 128:256],
                          ALU.mult),
                      reads=[b_gnrow], writes=[b_gnrow])
                def mkbc(eng):
                    eng.matmul(psS2[:, 256:384], lhsT=C.ones_row[0:1, :],
                               rhs=gnrow[0:1, 128:256], start=True, stop=True)
                    return eng.matmul(psS2[:, 384:512], lhsT=C.ones_row[0:1, :],
                                      rhs=gnrow[0:1, 0:128], start=True, stop=True)
                P.ins("t", mkbc, reads=[b_gnrow, C.b_ones], writes=[b_psS2])
                P.ins("v", lambda eng: eng.tensor_tensor(
                          ysb[:, :], ysb[:, :], psS2[:, 256:384], ALU.mult),
                      reads=[b_psS2, b_ysb], writes=[b_ysb])
                P.ins("v", lambda eng: eng.tensor_tensor(
                          ysb[:, :], ysb[:, :], psS2[:, 384:512], ALU.subtract),
                      reads=[b_psS2, b_ysb], writes=[b_ysb])
                P.fence("v", psS2[0:1, 256:257], b_psS2, C.fscr[0:1, 0:1])
                P.ins("v", lambda eng, h=h, li=li: eng.tensor_scalar(
                          ysb[:, :], ysb[:, :], gnwb[:, li, 0, h:h + 1],
                          gnwb[:, li, 1, h:h + 1], ALU.mult, ALU.add),
                      reads=[b_gnw, b_ysb], writes=[b_ysb])
                P.ins("v", lambda eng, h=h, ts2=ts2: eng.tensor_tensor(
                          gysb[ts2][0][:, h, :], ysb[:, :], gsil[:, h, :], ALU.mult),
                      reads=[b_ysb, b_gsil], writes=[gysb[ts2][1]])
            P.ins("y", lambda eng, tc=tc, ts2=ts2: eng.dma_start(
                      out=gyd[:, :, tc], in_=gysb[ts2][0][:, :, :]),
                  reads=[gysb[ts2][1]], writes=[b_gyd], sem=C.dY, inc=16)
        P.barrier()

        for tt in range(8):
            sl = slice(tt * 256, tt * 256 + 256)
            g2 = tt % 2
            P.ins("y", lambda eng, sl=sl, g2=g2: eng.dma_start(
                      out=gstr[g2][0][:, :, :], in_=gyd[:, :, sl]),
                  reads=[b_gyd], writes=[gstr[g2][1]], sem=(C.dH if g2 == 0 else C.dI),
                  inc=16)
            P.ins("y", lambda eng, sl=sl: eng.dma_start(out=xstr[:, :, :],
                                                        in_=xa[:, :, sl]),
                  reads=[b_xa], writes=[b_xstr], sem=C.dJ, inc=16)
            for f in range(NCH):
                ws = wos[f % 2]
                P.ins("g", lambda eng, f=f, li=li, ws=ws: eng.dma_start(
                          out=ws[0][:, :, :], in_=hb["a1_wo"][li, f].rearrange("k p f -> p k f")),
                      writes=[ws[1]], sem=(C.dA if f % 2 == 0 else C.dB), inc=16)
                def mko(eng, f=f, g2=g2, ws=ws):
                    last = None
                    for c in range(NCH):
                        last = eng.matmul(psY[:, (f % 2) * 256:(f % 2) * 256 + 256],
                                          lhsT=ws[0][:, c, :], rhs=gstr[g2][0][:, c, :],
                                          start=(c == 0), stop=(c == NCH - 1))
                    return last
                P.ins("t", mko, reads=[ws[1], gstr[g2][1]], writes=[b_psY])
                P.ins("v", lambda eng, f=f: eng.tensor_tensor(
                          xstr[:, f, :], xstr[:, f, :],
                          psY[:, (f % 2) * 256:(f % 2) * 256 + 256], ALU.add),
                      reads=[b_psY], writes=[b_xstr])
                P.fence("v", psY[0:1, (f % 2) * 256:(f % 2) * 256 + 1], b_psY,
                        C.fscr[0:1, 0:1])
            P.ins("y", lambda eng, sl=sl: eng.dma_start(out=ya[:, :, sl],
                                                        in_=xstr[:, :, :]),
                  reads=[b_xstr], writes=[b_ya], sem=C.dJ, inc=16)
        P.barrier()

        def p4_pre(tt, sl, w):
            P.ins("y", lambda eng, sl=sl: eng.dma_start(out=xstr[:, :, :],
                                                        in_=ya[:, :, sl]),
                  reads=[b_ya], writes=[b_xstr], sem=C.dJ, inc=16)
            return (lambda c: xstr[:, c, :]), [b_xstr] * NCH
        emit_ln(C, P, lnws, None, None,
                lambda c: xn[:, c, :], b_xn, pbc, n=T, tstep=256, pre_tile=p4_pre,
                gamma=lambda c, li=li: lnwb[:, li, 2, c:c + 1],
                beta=lambda c, li=li: lnwb[:, li, 3, c:c + 1], b_gb=b_lnw,
                out_local=False)
        P.barrier()
        P.ins("y", lambda eng, li=li: eng.dma_start(out=b1r[:], in_=hb["a1_b1"][li]),
              writes=[b_b1r], sem=C.dL, inc=16)
        for jb in range(4):
            P.dmas("g", [
                (lambda eng, j=j, li=li, jb=jb: eng.dma_start(
                    out=wpa[:, j, :, :],
                    in_=hb["a1_w1"][li, jb * 8 + j].rearrange("k p f -> p k f")))
                for j in range(8)
            ] + [
                (lambda eng, f=f, li=li, jb=jb: eng.dma_start(
                    out=wpb[:, f, :, :],
                    in_=hb["a1_w2"][li, f, jb * 8:jb * 8 + 8].rearrange("j p f -> p j f")))
                for f in range(8)
            ], writes=[b_wpa, b_wpb], sem=C.dC)
            for tt in range(8):
                sl = slice(tt * 256, tt * 256 + 256)
                for j in range(8):
                    jj = jb * 8 + j
                    def mk1(eng, j=j, sl=sl, jj=jj):
                        last = None
                        for c in range(NCH):
                            last = eng.matmul(psP[:, (j % 2) * 256:(j % 2) * 256 + 256],
                                              lhsT=wpa[:, j, c, :], rhs=xn[:, c, sl],
                                              start=(c == 0), stop=False)
                        return eng.matmul(psP[:, (j % 2) * 256:(j % 2) * 256 + 256],
                                          lhsT=b1r[0:1, jj * 128:jj * 128 + 128],
                                          rhs=C.onesb[0:1, 0:256], start=False, stop=True)
                    P.ins("t", mk1, reads=[b_wpa, b_b1r, C.b_ones] + b_xn, writes=[b_psP])
                    P.ins("s", lambda eng, j=j: eng.activation(
                              hbuf[:, j, :], psP[:, (j % 2) * 256:(j % 2) * 256 + 256],
                              AF.Gelu),
                          reads=[b_psP], writes=[b_hbuf])
                    P.fence("s", psP[0:1, (j % 2) * 256:(j % 2) * 256 + 1], b_psP,
                            C.fscr[0:1, 1:2])
                if dbg and li == nlayers - 1 and jb == 0 and tt == 0:
                    hbd = nc.dram_tensor("a1_hbd", [128, 8, 256], BF16,
                                         kind="ExternalOutput").ap()
                    P.ins("y", lambda eng, hbd=hbd: eng.dma_start(out=hbd, in_=hbuf[:, :, :]),
                          reads=[b_hbuf], sem=C.dY, inc=16)
                for f in range(NCH):
                    psW = psS2 if f % 2 == 0 else psY
                    bW = b_psS2 if f % 2 == 0 else b_psY
                    def mk2(eng, f=f, psW=psW):
                        last = None
                        for j in range(8):
                            last = eng.matmul(psW[:, 0:256], lhsT=wpb[:, f, j, :],
                                              rhs=hbuf[:, j, :], start=(j == 0),
                                              stop=(j == 7))
                        return last
                    P.ins("t", mk2, reads=[b_wpb, b_hbuf], writes=[bW])
                    if jb == 0:
                        P.ins("v", lambda eng, f=f, sl=sl, psW=psW: eng.tensor_copy(
                                  accf[:, f, sl], psW[:, 0:256]),
                              reads=[bW], writes=[b_big])
                    else:
                        P.ins("v", lambda eng, f=f, sl=sl, psW=psW: eng.tensor_tensor(
                                  accf[:, f, sl], accf[:, f, sl], psW[:, 0:256], ALU.add),
                              reads=[bW, b_big], writes=[b_big])
                    P.fence("v", psW[0:1, 0:1], bW, C.fscr[0:1, 0:1])
        if dbg and li == nlayers - 1:
            xnd = nc.dram_tensor("a1_xnd", [128, NCH, T], BF16, kind="ExternalOutput").ap()
            accd = nc.dram_tensor("a1_accd", [128, NCH, T], F32, kind="ExternalOutput").ap()
            P.ins("y", lambda eng, xnd=xnd: eng.dma_start(out=xnd, in_=xn[:]),
                  reads=b_xn, sem=C.dY, inc=16)
            P.ins("y", lambda eng, accd=accd: eng.dma_start(out=accd, in_=accf[:, :, 0:T]),
                  reads=[b_big], sem=C.dY, inc=16)
        P.barrier()
        for tt in range(8):
            sl = slice(tt * 256, tt * 256 + 256)
            P.ins("y", lambda eng, sl=sl: eng.dma_start(out=xstr[:, :, :],
                                                        in_=ya[:, :, sl]),
                  reads=[b_ya], writes=[b_xstr], sem=C.dJ, inc=16)
            for f in range(NCH):
                P.ins("v", lambda eng, f=f, sl=sl, li=li: eng.scalar_tensor_tensor(
                          xstr[:, f, :], accf[:, f, sl], b2c[:, li, f:f + 1],
                          xstr[:, f, :], ALU.add, ALU.add),
                      reads=[b_big, b_b2c], writes=[b_xstr])
            P.ins("y", lambda eng, sl=sl: eng.dma_start(out=xa[:, :, sl],
                                                        in_=xstr[:, :, :]),
                  reads=[b_xstr], writes=[b_xa], sem=C.dJ, inc=16)
        P.barrier()

    for tt in range(8):
        sl = slice(tt * 256, tt * 256 + 256)
        P.ins("y", lambda eng, sl=sl: eng.dma_start(out=xstr[:, :, :],
                                                    in_=xin_ap[:, :, sl]),
              reads=[b_xin], writes=[b_xstr], sem=C.dJ, inc=16)
        P.ins("y", lambda eng, sl=sl: eng.dma_start(out=accf[:, :, 0:256],
                                                    in_=xa[:, :, sl]),
              reads=[b_xa], writes=[b_big], sem=C.dJ, inc=16)
        for c in range(NCH):
            P.ins("v", lambda eng, c=c: eng.tensor_tensor(
                      xstr[:, c, :], xstr[:, c, :], accf[:, c, 0:256], ALU.add),
                  reads=[b_big, b_xstr], writes=[b_xstr])
        P.ins("y", lambda eng, sl=sl: eng.dma_start(out=xout_ap[:, :, sl],
                                                    in_=xstr[:, :, :]),
              reads=[b_xstr], writes=[b_xout], sem=C.dJ, inc=16)
    P.barrier()


def build_a1_launch(shared_host, percore_sample, dbg=False, nlayers=2):
    import contextlib
    nc = bass.Bass("TRN2", target_bir_lowering=False,
                   detect_race_conditions=RACE_DETECT)
    P = Prog(nc)
    st = contextlib.ExitStack()
    hbk = {}
    host = dict(shared_host)
    host.update(percore_sample)
    with st:
        C = alloc_common(nc, st, P)
        C.a1_lnws = alloc_lnws(C, st)
        for k, v in host.items():
            t = nc.dram_tensor(k, list(v.shape), _np2dt(v), kind="ExternalInput")
            hbk[k] = t.ap()
        xin = nc.dram_tensor("xin", [128, NCH, T], F32, kind="ExternalInput").ap()
        xout = nc.dram_tensor("xout", [128, NCH, T], F32, kind="ExternalOutput").ap()
        b_xin, b_xout = Buf("xin"), Buf("xout")
        load_common(C, P, hbk)
        emit_a1(C, P, st, hbk, xin, b_xin, xout, b_xout, dbg=dbg, nlayers=nlayers)
        with nc.Block() as block:
            P.replay(block)
    return nc



def _x_to_maps(xfull):
    maps = []
    for c in range(NCORES):
        xs = _fm_x(xfull[0, c * T:(c + 1) * T, :])
        maps.append(np.ascontiguousarray(xs.transpose(1, 0, 2)))
    return maps


def _x_from_results(res):
    out = np.zeros((1, NCORES * T, D), np.float32)
    for c in range(NCORES):
        xo = np.asarray(res.results[c]["xout"], np.float32)
        out[0, c * T:(c + 1) * T, :] = xo.transpose(1, 0, 2).reshape(D, T).T
    return out


def kernel(**inputs):
    from concourse.bass_utils import run_bass_kernel_spmd
    inp = {k: np.asarray(v) for k, v in inputs.items()}
    x = np.asarray(inp["L0_attn_x"], np.float32).reshape(1, NCORES * T, D)
    common = common_host_inputs()

    f0_sh = dict(common); f0_sh.update(f0_host_shared(inp))
    a0_sh = dict(common); a0_sh.update(a0_host_shared(inp))
    f1_sh = dict(common); f1_sh.update(f1_host_shared(inp))
    a1_sh = dict(common); a1_sh.update(a1_host_shared(inp))

    nc_f0 = build_f0_launch(f0_sh, f0_host_percore(0))
    nc_a0 = build_a0_launch(a0_sh, a0_host_percore(0))
    nc_f1 = build_f1_launch(f1_sh, {"pc_stflag": np.zeros((128, 8), np.float32)})
    nc_a1 = build_a1_launch(a1_sh, a1_host_percore(0))

    f1_pc = [{"pc_stflag": np.zeros((128, 8), np.float32)} for c in range(NCORES)]

    def run1(nc, sh, pcs, xfull):
        xmaps = _x_to_maps(xfull)
        in_maps = []
        for c in range(NCORES):
            m = dict(sh)
            m.update(pcs[c])
            m["xin"] = xmaps[c]
            in_maps.append(m)
        for attempt in range(3):
            res = run_bass_kernel_spmd(nc, in_maps, core_ids=list(range(NCORES)))
            nxt = _x_from_results(res)
            if np.isfinite(nxt).all():
                break
        return nxt

    ROT = 1024
    bad = np.array([c * T + i for c in range(1, NCORES) for i in (0, 1)])

    def run_f0(xfull):
        outA = run1(nc_f0, f0_sh, [f0_host_percore(c) for c in range(NCORES)], xfull)
        xrot = np.roll(xfull, -ROT, axis=1)
        outBr = run1(nc_f0, f0_sh,
                     [f0_host_percore(c, toff=ROT) for c in range(NCORES)], xrot)
        outB = np.roll(outBr, ROT, axis=1)
        outA[:, bad, :] = outB[:, bad, :]
        return outA

    cur = run_f0(x)
    cur = run1(nc_a0, a0_sh, [a0_host_percore(c) for c in range(NCORES)], cur)
    cur = run_f0(cur)
    cur = run1(nc_f1, f1_sh, f1_pc, cur)
    cur = run1(nc_a1, a1_sh, [a1_host_percore(c) for c in range(NCORES)], cur)
    cur = run1(nc_f1, f1_sh, f1_pc, cur)
    return cur
